# version 10
# baseline (speedup 1.0000x reference)
import numpy as np
from contextlib import ExitStack

import concourse.bass as bass
import concourse.mybir as mybir
from concourse.bass_utils import run_bass_kernel_spmd

F32 = mybir.dt.float32
BF16 = mybir.dt.bfloat16
ALU = mybir.AluOpType
ACTF = mybir.ActivationFunctionType
AX = mybir.AxisListType.X

B, S, D = 4, 4096, 2048
NH, NKV, HD = 16, 4, 128
NJOB = 16
NDC = D // 128
SM_SCALE = float(np.float32(1.0 / np.sqrt(128.0)))
LN_EPS = 1e-5
LN_MODE = 2
N_LN_TILES = 32
RMS_EPS = 1e-6
NEG_BIG = -3.0e38
MASK_ADD = -1.0e9


def job_class(role, job):
    j, s = job // 2, job % 2
    return 4 * j + ((3 if s else 0) if role == 0 else (2 if s else 1))


def nkc_of(job):
    return job // 2 + 1


def nkt_of(job):
    j, s = job // 2, job % 2
    return 4 * j + (4 if s else 2)


def _copy(e, out, in_):
    if hasattr(e, "tensor_copy"):
        return e.tensor_copy(out, in_)
    return e.copy(out, in_)


class Sched:
    ENGINES = ("pe", "dve", "act", "sync", "gp")

    def __init__(self, nc):
        self.nc = nc
        self.ops = {e: [] for e in self.ENGINES}
        self.kinds = {e: [] for e in self.ENGINES}
        self.cnt = {}
        self.sem = {}

    def new_sem(self, ctx, name):
        self.sem[name] = ctx.enter_context(self.nc.semaphore(name))
        self.cnt[name] = 0

    def emit(self, eng, fn, kind="op"):
        self.ops[eng].append(fn)
        self.kinds[eng].append(kind)

    def inc(self, eng, name, by=1):
        sem = self.sem[name]
        self.cnt[name] += by
        if self.kinds[eng] and self.kinds[eng][-1] == "op":
            last = self.ops[eng].pop()
            self.kinds[eng].pop()
            self.emit(eng, lambda e, f=last: f(e).then_inc(sem, by),
                      kind="op+inc")
        else:
            self.emit(eng, lambda e: e.drain().then_inc(sem, by),
                      kind="op+inc")
        return self.cnt[name]

    def wait(self, eng, name, value=None):
        if value is None:
            value = self.cnt[name]
        if value <= 0:
            return
        assert value <= self.cnt[name], (name, value, self.cnt[name])
        sem = self.sem[name]
        self.emit(eng, lambda e: e.wait_ge(sem, value), kind="wait")

    def dma(self, eng, name, out, in_):
        sem = self.sem[name]
        self.cnt[name] += 16
        self.emit(eng,
                  lambda e: e.dma_start(out=out, in_=in_).then_inc(sem, 16),
                  kind="op+inc")
        return self.cnt[name]

    def rt(self, eng):
        v = self.inc(eng, "rt_" + eng)
        self.wait(eng, "rt_" + eng, v)

    def barrier(self):
        dma_sems = [n for n in self.cnt if n.startswith("d")]
        for e in self.ENGINES:
            if e in ("sync", "gp"):
                for s_ in dma_sems:
                    self.wait(e, s_, self.cnt[s_])
            self.emit(e, lambda en, s=self.sem["b_" + e]:
                      en.drain().then_inc(s, 1), kind="op+inc")
            self.cnt["b_" + e] += 1
        tgt = {e: self.cnt["b_" + e] for e in self.ENGINES}
        for e in self.ENGINES:
            for o in self.ENGINES:
                if o != e:
                    self.wait(e, "b_" + o, tgt[o])


class SbufAlloc:
    def __init__(self, nc, base, limit):
        self.nc = nc
        self.limit = limit
        self.off = base
        self.n = 0
        self.offs = {}

    def tile(self, shape, dtype=F32):
        per_part = int(np.prod(shape[1:])) * mybir.dt.size(dtype)
        per_part = (per_part + 31) // 32 * 32
        ap = self.tile_at(self.off, shape, dtype)
        self.off += per_part
        return ap

    def tile_at(self, off, shape, dtype=F32):
        per_part = int(np.prod(shape[1:])) * mybir.dt.size(dtype)
        assert off + per_part <= self.limit, (
            f"SBUF over budget: {off}+{per_part} > {self.limit}")
        h = self.nc.alloc_sbuf_tensor_at(
            f"sb{self.n}", list(shape), dtype, offset=off)
        ap = h.ap()
        self.offs[id(ap)] = off
        self.n += 1
        return ap

    def off_of(self, ap):
        return self.offs[id(ap)]

    def scope(self):
        return self.off

    def reset(self, off):
        self.off = off


def build_program(debug=False,
                  phases=("ps", "p0", "p0b", "pq", "pa", "pw", "pf")):
    nc = bass.Bass("TRN2", target_bir_lowering=False)

    def din(name, shape):
        return nc.dram_tensor(name, list(shape), F32, kind="ExternalInput").ap()

    x_full = din("x_full", (S, D))
    x_own = din("x_own", (NJOB, 128, D))
    masks = din("masks", (NJOB, 128, 512))
    wq_r = din("wq_r", (NH, D, HD))
    wk_r = din("wk_r", (NKV, D, HD))
    wv_r = din("wv_r", (128, NDC, NKV * HD))
    wo_r = din("wo_r", (NDC, 128, D))
    wr_r = din("wr_r", (128, NDC, 8))
    ew_r = din("ew_r", (8, 4, 128, 4, NDC, 128))
    pr_r = din("pr_r", (NDC, 128, D))
    eb_pp = din("eb_pp", (128, 8, NDC))
    esel = din("esel", (128, 8, 128))
    bq_p = din("bq_p", (128, NH))
    bk_p = din("bk_p", (128, NKV))
    bv_b = din("bv_b", (128, NKV * HD))
    bo_b = din("bo_b", (128, D))
    pb_b = din("pb_b", (128, D))
    nw_b = din("nw_b", (128, D))
    rb_p = din("rb_p", (128, 1))
    ident_in = din("ident", (128, 128))
    identb_in = nc.dram_tensor("identb", [128, 128], BF16,
                               kind="ExternalInput").ap()
    cpool_in = din("cpool", (128, 8))

    out_own = nc.dram_tensor("out_own", [NJOB, 128, D], F32,
                             kind="ExternalOutput").ap()

    skind = "ExternalOutput" if debug else "Internal"

    def scr(name, shape):
        return nc.dram_tensor(name, list(shape), F32, kind=skind).ap()

    def scrb(name, shape):
        return nc.dram_tensor(name, list(shape), BF16, kind=skind).ap()

    ewh_s = scrb("ewh_s", (8, 4, 128, 4 * NDC * 128))
    ewl_s = scrb("ewl_s", (8, 4, 128, 4 * NDC * 128))
    xnh_s = scrb("xnh_s", (NJOB, NDC, 128, 128))
    xnl_s = scrb("xnl_s", (NJOB, NDC, 128, 128))
    k_s = scr("k_s", (NKV, 128, S))
    v_s = scr("v_s", (S // 128, 128, NKV * HD))
    xnofm = scr("xnofm", (NJOB, NDC, 128, 128))
    q_s = scr("q_s", (NH, 128, 2048))
    at_s = scr("at_s", (NJOB, 128, D))
    af_s = scr("af_s", (NJOB, 128, D))

    with ExitStack() as ctx:
        sc = Sched(nc)
        semnames = (["dC", "dMISC", "dXQ", "dWkv", "dKVL", "dWO", "dPR",
                     "dTL", "dO", "cP", "cV", "cA",
                     "rt_dve", "rt_act", "rt_pe",
                     "b_pe", "b_dve", "b_act", "b_sync", "b_gp"]
                    + [f"dX{i}" for i in range(4)]
                    + [f"{p}{i}" for p in ("dKk", "dKv", "dXN", "dWQ", "dQS",
                                           "dQJ", "dAO", "dAT", "dAF", "dWE")
                       for i in range(2)])
        for s_ in semnames:
            sc.new_sem(ctx, s_)

        pbk = [nc.alloc_psum_tensor(f"ps{i}", [128, 512], F32).ap()
               for i in range(8)]

        al = SbufAlloc(nc, (nc._sbuf_addr_for_side('left') + 31) // 32 * 32,
               nc._sbuf_addr_for_side('right') - 64)

        c_ident = al.tile([128, 128])
        c_identb = al.tile([128, 128], BF16)
        c_esel = al.tile([128, 8, 128])
        c_pool = al.tile([128, 8])
        c_bq = al.tile([128, NH])
        c_bk = al.tile([128, NKV])
        c_rb = al.tile([128, 1])
        c_bvb = al.tile([128, NKV * HD])
        c_ebpp = al.tile([128, 8, NDC])
        c_zmask = al.tile([128, 512])
        PERSIST = al.scope()

        eps_ln = c_pool[:, 0:1]
        eps_rms = c_pool[:, 1:2]
        neg_init = c_pool[:, 2:3]

        for dst, src in [(c_ident, ident_in), (c_identb, identb_in),
                         (c_esel, esel),
                         (c_pool, cpool_in), (c_bq, bq_p), (c_bk, bk_p),
                         (c_rb, rb_p), (c_bvb, bv_b), (c_ebpp, eb_pp)]:
            sc.dma("sync", "dC", dst[:], src)
        sc.wait("dve", "dC")
        sc.emit("dve", lambda e: e.memset(c_zmask[:, :], 0.0))
        sc.barrier()

        def transpose_2048(src_col_fn, dst_fn, banks, evac_eng, state):
            ev = state.setdefault("ev", [])
            nm = "cA" if evac_eng == "act" else "cV"
            for half in range(4):
                bank = banks[half % 2]
                if len(ev) >= 2:
                    sc.wait("pe", nm, ev[-2])
                for q in range(4):
                    dc = half * 4 + q
                    sc.emit("pe", lambda e, d=dc, q2=q, b=bank:
                            e.transpose(b[:, q2 * 128:(q2 + 1) * 128],
                                        src_col_fn(d), c_ident[:, :]))
                p1 = sc.inc("pe", "cP")
                sc.wait(evac_eng, "cP", p1)
                for q in range(4):
                    dc = half * 4 + q
                    sc.emit(evac_eng, lambda e, d=dc, q2=q, b=bank:
                            _copy(e, dst_fn(d),
                                  b[:, q2 * 128:(q2 + 1) * 128]))
                ev.append(sc.inc(evac_eng, nm))
            return ev[-1]

        sp_state = {"v": [], "st": [], "ld": []}
        sp_bufs = {}

        def make_split_feeders():
            w32 = sp_bufs["w32"]
            whb = sp_bufs["whb"]
            wlb = sp_bufs["wlb"]

            def chunk(ci):
                e_, fcg = ci // 4, ci % 4
                if sp_state["v"]:
                    sc.wait("sync", "cV", sp_state["v"][-1])
                wl = sc.dma("sync", "dWE0",
                            w32[:, :],
                            ew_r[e_, fcg].rearrange("p a b c -> p (a b c)"))
                sc.wait("dve", "dWE0", wl)
                if sp_state["st"]:
                    sc.wait("dve", "dXN0", sp_state["st"][-1])
                sc.emit("dve", lambda e:
                        e.tensor_copy(whb[:, :], w32[:, :]))
                sc.rt("dve")
                sc.emit("dve", lambda e:
                        e.tensor_tensor(wlb[:, :], w32[:, :],
                                        whb[:, :], op=ALU.subtract))
                sp_state["v"].append(sc.inc("dve", "cV"))
                sc.wait("gp", "cV", sp_state["v"][-1])
                sc.dma("gp", "dXN0", ewh_s[e_, fcg], whb[:, :])
                sp_state["st"].append(sc.dma("gp", "dXN0", ewl_s[e_, fcg],
                                             wlb[:, :]))

            return [(lambda ci=ci: chunk(ci)) for ci in range(32)]

        def emit_ln(src_fn, ntiles, xnfm_g, gcols, sink_fn, sink_every,
                    pre_evac_wait=None):
            ph = al.scope()
            xbuf = [al.tile([128, D]) for _ in range(4)]
            sqj = al.tile([128, D])
            xnb = [al.tile([128, D]) for _ in range(2)]
            st = {n: [al.tile([128, 1]) for _ in range(4)]
                  for n in ("sx", "sq", "m", "va", "sd", "rstd", "nmr", "tmp")}
            apply_done = {}
            tr_done = {}
            trst = {}
            self_sq = [0]

            def stats(t, ld_val):
                i = t % 4
                sx, sq_, m_, va, sd_, rstd, nmr, tmp = (
                    st[n][i] for n in ("sx", "sq", "m", "va", "sd",
                                       "rstd", "nmr", "tmp"))
                sc.wait("dve", f"dX{i}", ld_val)
                sc.emit("dve", lambda e: e.reduce_sum(sx, xbuf[i][:, :], axis=AX))
                sc.rt("dve")
                sc.wait("act", f"dX{i}", ld_val)
                sc.wait("act", "cA", self_sq[0])
                sc.emit("act", lambda e: e.activation(
                    sqj[:, :], xbuf[i][:, :], ACTF.Square, accum_out=sq_))
                a1 = sc.inc("act", "cA")
                self_sq[0] = a1
                sc.wait("dve", "cA", a1)
                sc.emit("dve", lambda e: e.tensor_scalar(
                    m_, sx, 1.0 / D, None, op0=ALU.mult))
                sc.rt("dve")
                sc.emit("dve", lambda e: e.tensor_tensor(tmp, m_, m_, op=ALU.mult))
                sc.emit("dve", lambda e: e.tensor_scalar(
                    va, sq_, 1.0 / D, None, op0=ALU.mult))
                sc.rt("dve")
                sc.emit("dve", lambda e: e.tensor_tensor(
                    va, va, tmp, op=ALU.subtract))
                v1 = sc.inc("dve", "cV")
                sc.wait("act", "cV", v1)
                sc.emit("act", lambda e: e.activation(
                    sd_, va, ACTF.Sqrt, bias=eps_ln))
                a2 = sc.inc("act", "cA")
                sc.wait("dve", "cA", a2)
                sc.emit("dve", lambda e: e.reciprocal(rstd, sd_))
                sc.rt("dve")
                sc.emit("dve", lambda e: e.tensor_tensor(
                    nmr, m_, rstd, op=ALU.mult))
                sc.rt("dve")
                sc.emit("dve", lambda e: e.tensor_scalar(
                    nmr, nmr, -1.0, None, op0=ALU.mult))
                return sc.inc("dve", "cV")

            def apply_tr(t, stats_ready):
                i = t % 4
                rstd, nmr = st["rstd"][i], st["nmr"][i]
                sc.wait("act", "cV", stats_ready)
                if LN_MODE >= 1:
                    sc.emit("act", lambda e: e.activation(
                        xnb[t % 2][:, :], xbuf[i][:, :], ACTF.Identity,
                        bias=nmr, scale=rstd))
                else:
                    sc.emit("act", lambda e: _copy(
                        e, xnb[t % 2][:, :], xbuf[i][:, :]))
                apply_done[t] = sc.inc("act", "cA")
                if pre_evac_wait is not None:
                    pre_evac_wait(t)
                if LN_MODE == 1 or LN_MODE == 0:
                    tr_done[t] = apply_done[t]
                    return
                sc.wait("pe", "cA", apply_done[t])
                col = t % gcols
                tr_done[t] = transpose_2048(
                    lambda dc: xnb[t % 2][:, dc * 128:(dc + 1) * 128],
                    lambda dc: xnfm_g[:, dc, col * 128:(col + 1) * 128],
                    (pbk[0], pbk[1]), "act", trst)

            pend = []
            for t in range(ntiles + 2):
                if t < ntiles:
                    if t >= 4:
                        sc.wait("sync", "cA", apply_done[t - 4])
                    ld = sc.dma("sync", f"dX{t % 4}", xbuf[t % 4][:, :],
                                src_fn(t))
                    pend.append((t, stats(t, ld)))
                if len(pend) > 2 or (t >= ntiles and pend):
                    pt, pr = pend.pop(0)
                    apply_tr(pt, pr)
                    if (pt + 1) % sink_every == 0:
                        sink_fn(pt // sink_every, tr_done[pt])
            al.reset(ph)

        if "p0" in phases or "p0ln" in phases:
            ph0 = al.scope()
            xnfm_g = al.tile([128, NDC, 512])
            wk_sb = al.tile([128, NKV * NDC, HD])
            wv_sb = al.tile([128, NDC, NKV * HD])
            kst = [al.tile([128, 512]) for _ in range(2)]
            vst = [al.tile([128, 512]) for _ in range(2)]

            for fc in range(NKV):
                for dc in range(NDC):
                    sc.dma("sync", "dWkv", wk_sb[:, fc * NDC + dc, :],
                           wk_r[fc, dc * 128:(dc + 1) * 128, :])
            sc.dma("sync", "dWkv", wv_sb[:, :, :], wv_r)
            w_ld = sc.cnt["dWkv"]

            kv_ev, v_ev, k_st, v_st = [], [], [], []

            def kv_sink(g, tr_ready):
                sc.wait("pe", "cA", tr_ready)
                if g == 0:
                    sc.wait("pe", "dWkv", w_ld)
                for fc in range(NKV):
                    bank = pbk[2 + fc % 2]
                    if len(kv_ev) >= 2:
                        sc.wait("pe", "cA", kv_ev[-2])
                    for dc in range(NDC):
                        sc.emit("pe", lambda e, f=fc, d=dc, b=bank:
                                e.matmul(b[:, :], wk_sb[:, f * NDC + d, :],
                                         xnfm_g[:, d, :],
                                         start=(d == 0), stop=(d == NDC - 1)))
                    p1 = sc.inc("pe", "cP")
                    sc.wait("act", "cP", p1)
                    if len(k_st) >= 2:
                        sc.wait("act", f"dKk{fc % 2}", k_st[-2][1])
                    kb = kst[fc % 2]
                    sc.emit("act", lambda e, b=bank, k=kb, f=fc:
                            e.activation(k[:, :], b[:, :], ACTF.Identity,
                                         bias=c_bk[:, f:f + 1]))
                    kv_ev.append(sc.inc("act", "cA"))
                    sc.wait("gp", "cA", kv_ev[-1])
                    k_st.append((fc % 2, sc.dma(
                        "gp", f"dKk{fc % 2}",
                        k_s[fc, :, g * 512:(g + 1) * 512], kb[:, :])))
                for kt in range(4):
                    bank = pbk[4 + kt % 2]
                    if len(v_ev) >= 2:
                        sc.wait("pe", "cV", v_ev[-2])
                    for dc in range(NDC):
                        sc.emit("pe", lambda e, k=kt, d=dc, b=bank:
                                e.matmul(b[:, :],
                                         xnfm_g[:, d, k * 128:(k + 1) * 128],
                                         wv_sb[:, d, :],
                                         start=(d == 0), stop=(d == NDC - 1)))
                    p1 = sc.inc("pe", "cP")
                    sc.wait("dve", "cP", p1)
                    if len(v_st) >= 2:
                        sc.wait("dve", f"dKv{kt % 2}", v_st[-2][1])
                    vb = vst[kt % 2]
                    sc.emit("dve", lambda e, b=bank, v=vb:
                            e.scalar_tensor_tensor(v[:, :], b[:, :], 1.0,
                                                   c_bvb[:, :],
                                                   op0=ALU.mult, op1=ALU.add))
                    v_ev.append(sc.inc("dve", "cV"))
                    sc.wait("gp", "cV", v_ev[-1])
                    v_st.append((kt % 2, sc.dma(
                        "gp", f"dKv{kt % 2}", v_s[g * 4 + kt, :, :], vb[:, :])))

            if "p0ln" in phases:
                emit_ln(lambda t: x_full[t * 128:(t + 1) * 128, :],
                        N_LN_TILES, xnfm_g, 4, lambda g, tr: None, 4)
            else:
                emit_ln(lambda t: x_full[t * 128:(t + 1) * 128, :],
                        32, xnfm_g, 4, kv_sink, 4)
            al.reset(ph0)
            sc.barrier()

        if "p0b" in phases:
            ph = al.scope()
            xnfm_1 = al.tile([128, NDC, 128])
            xnfm1h = al.tile([128, NDC, 128], BF16)
            xnfm1l = al.tile([128, NDC, 128], BF16)
            own_st = {}
            own_sp = {}

            def own_pre_evac(t):
                if t - 1 in own_st:
                    sc.wait("act", f"dXN{(t - 1) % 2}", own_st[t - 1])
                if t - 1 in own_sp:
                    sc.wait("act", "cV", own_sp[t - 1])

            def own_sink(t, tr_ready):
                sc.wait("dve", "cA", tr_ready)
                if t - 1 in own_st:
                    sc.wait("dve", f"dXN{(t - 1) % 2}", own_st[t - 1])
                fm_flat = xnfm_1.rearrange("p a b -> p (a b)")
                h_flat = xnfm1h.rearrange("p a b -> p (a b)")
                l_flat = xnfm1l.rearrange("p a b -> p (a b)")
                sc.emit("dve", lambda e: e.tensor_copy(h_flat, fm_flat))
                sc.rt("dve")
                sc.emit("dve", lambda e:
                        e.tensor_tensor(l_flat, fm_flat, h_flat,
                                        op=ALU.subtract))
                own_sp[t] = sc.inc("dve", "cV")
                sc.wait("gp", "cA", tr_ready)
                sc.wait("gp", "cV", own_sp[t])
                for dc in range(NDC):
                    sc.dma("gp", f"dXN{t % 2}", xnofm[t, dc, :, :],
                           xnfm_1[:, dc, :])
                    sc.dma("gp", f"dXN{t % 2}", xnh_s[t, dc, :, :],
                           xnfm1h[:, dc, :])
                    sc.dma("gp", f"dXN{t % 2}", xnl_s[t, dc, :, :],
                           xnfm1l[:, dc, :])
                own_st[t] = sc.cnt[f"dXN{t % 2}"]

            emit_ln(lambda t: x_own[t], NJOB, xnfm_1, 1, own_sink, 1,
                    pre_evac_wait=own_pre_evac)
            al.reset(ph)
            sc.barrier()

        if "pq" in phases:
            ph = al.scope()
            xnq = al.tile([128, NDC, 512])
            wq_sb = [al.tile([128, NDC, HD]) for _ in range(2)]
            qst = [al.tile([128, 512]) for _ in range(2)]
            sp_bufs["w32"] = al.tile([128, 8192])
            sp_bufs["whb"] = al.tile([128, 8192], BF16)
            sp_bufs["wlb"] = al.tile([128, 8192], BF16)
            feeders = make_split_feeders() if "ps" in phases else []
            qev, qsd, qp1 = [], [], []
            for qg in range(4):
                if qp1:
                    sc.wait("sync", "cP", qp1[-1])
                lds = 0
                for jj in range(4):
                    job = qg * 4 + jj
                    for dc in range(NDC):
                        lds = sc.dma("sync", "dXQ",
                                     xnq[:, dc, jj * 128:(jj + 1) * 128],
                                     xnofm[job, dc, :, :])
                for fc in range(NH):
                    fci = qg * NH + fc
                    if fci % 2 == 0 and feeders:
                        feeders.pop(0)()
                    wb = wq_sb[fci % 2]
                    if len(qp1) >= 2:
                        sc.wait("sync", "cP", qp1[-2])
                    wl = 0
                    for dc in range(NDC):
                        wl = sc.dma("sync", f"dWQ{fci % 2}", wb[:, dc, :],
                                    wq_r[fc, dc * 128:(dc + 1) * 128, :])
                    bank = pbk[fci % 2]
                    sc.wait("pe", f"dWQ{fci % 2}", wl)
                    sc.wait("pe", "dXQ", lds)
                    if len(qev) >= 2:
                        sc.wait("pe", "cA", qev[-2])
                    for dc in range(NDC):
                        sc.emit("pe", lambda e, d=dc, b=bank, w=wb:
                                e.matmul(b[:, :], w[:, d, :], xnq[:, d, :],
                                         start=(d == 0), stop=(d == NDC - 1)))
                    qp1.append(sc.inc("pe", "cP"))
                    sc.wait("act", "cP", qp1[-1])
                    if len(qsd) >= 2:
                        sc.wait("act", f"dQS{fci % 2}", qsd[-2][1])
                    qb = qst[fci % 2]
                    sc.emit("act", lambda e, b=bank, q=qb, f=fc:
                            e.activation(q[:, :], b[:, :], ACTF.Identity,
                                         bias=c_bq[:, f:f + 1]))
                    qev.append(sc.inc("act", "cA"))
                    sc.wait("gp", "cA", qev[-1])
                    qsd.append((fci % 2, sc.dma(
                        "gp", f"dQS{fci % 2}",
                        q_s[fc, :, qg * 512:(qg + 1) * 512], qb[:, :])))
            al.reset(ph)
            sc.barrier()

        if "pa" in phases:
            ph = al.scope()
            K_h = al.tile([128, S])
            V_h = al.tile([128, S // 128, HD])
            qjob = [al.tile([128, 4, HD]) for _ in range(2)]
            mbuf = [al.tile([128, 512]) for _ in range(2)]
            P_b = [al.tile([128, 4, S]) for _ in range(2)]
            mx = al.tile([128, 4, 8])
            negmax = [al.tile([128, 4]) for _ in range(2)]
            sumex = [al.tile([128, 4]) for _ in range(2)]
            rs_t = [al.tile([128, 4]) for _ in range(2)]
            pt_sb = [al.tile([128, 512]) for _ in range(2)]
            ast = [al.tile([128, 512]) for _ in range(2)]
            rsT_sb = [al.tile([128, 128]) for _ in range(2)]
            rsb_sb = [al.tile([128, 512]) for _ in range(2)]
            jnk_a = al.tile([128, 512])

            for kvh in range(NKV):
                kl = sc.dma("sync", "dKVL", K_h[:, :], k_s[kvh])
                for kt in range(S // 128):
                    kl = sc.dma("sync", "dKVL", V_h[:, kt, :],
                                v_s[kt, :, kvh * HD:(kvh + 1) * HD])
                q_lds, m_lds = {}, {}
                sv_ev, exp_done, rs_done = {}, {}, {}
                av_done, ao_ev, ao_st, sc_p1, rs_ev = {}, {}, {}, {}, {}

                def load_job(job):
                    i = job % 2
                    if job >= 2:
                        sc.wait("sync", "cP", sc_p1[job - 2])
                        sc.wait("sync", "cV", sv_ev[job - 2])
                    ql = 0
                    for h in range(4):
                        ql = sc.dma("sync", f"dQJ{i}", qjob[i][:, h, :],
                                    q_s[kvh * 4 + h, :,
                                        job * 128:(job + 1) * 128])
                    q_lds[job] = ql
                    m_lds[job] = sc.dma("sync", f"dQJ{i}", mbuf[i][:, :],
                                        masks[job])

                def emit_scores(job):
                    i = job % 2
                    nkc = nkc_of(job)
                    sc.wait("pe", f"dQJ{job % 2}", m_lds[job])
                    if job == 0:
                        sc.wait("pe", "dKVL", kl)
                    if job >= 1:
                        sc.wait("pe", "cV", sv_ev[job - 1])
                    if job >= 2:
                        sc.wait("pe", "cP", av_done[job - 2])
                    evs = []
                    for h in range(4):
                        for kc in range(nkc):
                            bank = pbk[(h * nkc + kc) % 2]
                            if len(evs) >= 2:
                                sc.wait("pe", "cV", evs[-2])
                            sc.emit("pe", lambda e, h2=h, k2=kc, b=bank, i2=i:
                                    e.matmul(b[:, :], qjob[i2][:, h2, :],
                                             K_h[:, k2 * 512:(k2 + 1) * 512],
                                             start=True, stop=True,
                                             skip_group_check=True))
                            p1 = sc.inc("pe", "cP")
                            sc.wait("dve", "cP", p1)
                            msk = mbuf[i] if kc == nkc - 1 else c_zmask
                            sc.emit("dve", lambda e, h2=h, k2=kc, b=bank,
                                    m2=msk, i2=i:
                                    e.scalar_tensor_tensor(
                                        P_b[i2][:, h2, k2 * 512:(k2 + 1) * 512],
                                        b[:, :], SM_SCALE, m2[:, :],
                                        op0=ALU.mult, op1=ALU.add))
                            evs.append(sc.inc("dve", "cV"))
                    sc.rt("dve")
                    for h in range(4):
                        for kc in range(nkc):
                            sc.emit("dve", lambda e, h2=h, k2=kc, i2=i:
                                    e.reduce_max(
                                        mx[:, h2, k2:k2 + 1],
                                        P_b[i2][:, h2, k2 * 512:(k2 + 1) * 512],
                                        axis=AX))
                    sc.rt("dve")
                    for h in range(4):
                        sc.emit("dve", lambda e, h2=h, n2=nkc, i2=i:
                                e.reduce_max(negmax[i2][:, h2:h2 + 1],
                                             mx[:, h2, 0:n2], axis=AX,
                                             negate=True))
                    sc_p1[job] = sc.cnt["cP"]
                    sv_ev[job] = sc.inc("dve", "cV")

                def emit_exps(job):
                    i = job % 2
                    nkc = nkc_of(job)
                    sc.wait("act", "cV", sv_ev[job])
                    if job <= 1:
                        for q_ in range(4):
                            sc.emit("act", lambda e, q2=q_: _copy(
                                e, jnk_a[:, q2 * 128:(q2 + 1) * 128],
                                c_zmask[:, 0:128]))
                    for h in range(4):
                        sc.emit("act", lambda e, h2=h, i2=i, n2=nkc:
                                e.activation(P_b[i2][:, h2, 0:n2 * 512],
                                             P_b[i2][:, h2, 0:n2 * 512],
                                             ACTF.Exp,
                                             bias=negmax[i2][:, h2:h2 + 1],
                                             accum_out=sumex[i2][:, h2:h2 + 1]))
                    exp_done[job] = sc.inc("act", "cA")
                    sc.wait("dve", "cA", exp_done[job])
                    sc.emit("dve", lambda e, i2=i:
                            e.reciprocal(rs_t[i2][:, :], sumex[i2][:, :]))
                    rs_done[job] = sc.inc("dve", "cV")

                def emit_av(job):
                    i = job % 2
                    nkt = nkt_of(job)
                    obank = pbk[4 + job % 2]
                    sc.wait("pe", "cA", exp_done[job])
                    if job >= 2:
                        sc.wait("pe", "cV", ao_ev[job - 2])
                    ptev = []

                    def av_tr(kt):
                        tbank = pbk[2 + kt % 2]
                        if kt >= 2:
                            sc.wait("pe", "cA", ptev[kt - 2])
                        for h in range(4):
                            sc.emit("pe", lambda e, h2=h, k2=kt, b=tbank,
                                    i2=i:
                                    e.transpose(
                                        b[:, h2 * 128:(h2 + 1) * 128],
                                        P_b[i2][:, h2,
                                                k2 * 128:(k2 + 1) * 128],
                                        c_ident[:, :]))
                        p1 = sc.inc("pe", "cP")
                        sc.wait("act", "cP", p1)
                        pt = pt_sb[kt % 2]
                        sc.emit("act", lambda e, b=tbank, p2=pt:
                                _copy(e, p2[:, :], b[:, :]))
                        ptev.append(sc.inc("act", "cA"))

                    av_tr(0)
                    for kt in range(nkt):
                        if kt + 1 < nkt:
                            av_tr(kt + 1)
                        pt = pt_sb[kt % 2]
                        sc.wait("pe", "cA", ptev[kt])
                        sc.emit("pe", lambda e, k2=kt, p2=pt, b=obank, n2=nkt:
                                e.matmul(b[:, :], V_h[:, k2, :], p2[:, :],
                                         start=(k2 == 0),
                                         stop=(k2 == n2 - 1),
                                         skip_group_check=True))
                    av_done[job] = sc.inc("pe", "cP")
                    sc.wait("pe", "cV", rs_done[job])
                    if job >= 1 and (job - 1) in rs_ev:
                        sc.wait("pe", "cV", rs_ev[job - 1])
                    sc.emit("pe", lambda e, i2=i:
                            e.transpose(pbk[6][:4, 0:128], rs_t[i2][:, :],
                                        c_ident[:, :]))
                    pt1 = sc.inc("pe", "cP")
                    sc.wait("dve", "cP", pt1)
                    sc.emit("dve", lambda e, i2=i:
                            _copy(e, rsT_sb[i2][:4, :], pbk[6][:4, 0:128]))
                    rtv = sc.inc("dve", "cV")
                    sc.wait("pe", "cV", rtv)
                    for h in range(4):
                        sc.emit("pe", lambda e, h2=h, i2=i:
                                e.matmul(pbk[7][:, h2 * 128:(h2 + 1) * 128],
                                         c_esel[0:4, h2, :],
                                         rsT_sb[i2][:4, :],
                                         start=(h2 == 0), stop=(h2 == 3),
                                         skip_group_check=True))
                    pb2 = sc.inc("pe", "cP")
                    sc.wait("dve", "cP", pb2)
                    sc.emit("dve", lambda e, i2=i:
                            _copy(e, rsb_sb[i2][:, :], pbk[7][:, :]))
                    rbv = sc.inc("dve", "cV")
                    rs_ev[job] = rbv
                    sc.wait("dve", "cP", av_done[job])
                    sc.wait("dve", "cV", rbv)
                    if job >= 2:
                        sc.wait("dve", f"dAO{job % 2}", ao_st[job - 2])
                    ab = ast[job % 2]
                    sc.emit("dve", lambda e, b=obank, a=ab, i2=i:
                            e.tensor_tensor(a[:, :], b[:, :],
                                            rsb_sb[i2][:, :], op=ALU.mult))
                    ao_ev[job] = sc.inc("dve", "cV")
                    sc.wait("gp", "cV", ao_ev[job])
                    ao_st[job] = sc.dma(
                        "gp", f"dAO{job % 2}",
                        at_s[job, :, kvh * 512:(kvh + 1) * 512], ab[:, :])

                load_job(0)
                for job in range(NJOB):
                    emit_scores(job)
                    if job + 1 < NJOB:
                        load_job(job + 1)
                    if job >= 1:
                        emit_av(job - 1)
                    emit_exps(job)
                emit_av(NJOB - 1)
                sc.barrier()
            al.reset(ph)

        if "pw" in phases:
            ph = al.scope()
            wo_sb = al.tile([128, NDC, 512])
            bo_sb = al.tile([128, D])
            atb = [al.tile([128, D]) for _ in range(2)]
            afm = al.tile([128, NDC, 128])
            wst = [al.tile([128, 512]) for _ in range(2)]
            bol = sc.dma("sync", "dMISC", bo_sb[:, :], bo_b)
            for fg in range(4):
                wl = 0
                for dc in range(NDC):
                    wl = sc.dma("sync", "dWO", wo_sb[:, dc, :],
                                wo_r[dc, :, fg * 512:(fg + 1) * 512])
                wev, wsd, trl = [], [], {}
                trst = {}
                for job in range(NJOB):
                    it = job % 2
                    if job >= 2:
                        sc.wait("sync", "cP", trl[job - 2])
                    a_ld = sc.dma("sync", f"dAT{it}", atb[it][:, :],
                                  at_s[job])
                    sc.wait("pe", f"dAT{it}", a_ld)
                    bank = pbk[2 + job % 2]
                    if job == 0:
                        sc.wait("pe", "dWO", wl)
                        sc.wait("pe", "dMISC", bol)
                    if len(wev) >= 2:
                        sc.wait("pe", "cV", wev[-2])
                    for dc in range(NDC):
                        sc.emit("pe", lambda e, d=dc, b=bank, i2=it:
                                e.matmul(b[:, :],
                                         atb[i2][:, d * 128:(d + 1) * 128],
                                         wo_sb[:, d, :],
                                         start=(d == 0), stop=(d == NDC - 1)))
                    p1 = sc.inc("pe", "cP")
                    trl[job] = p1
                    sc.wait("dve", "cP", p1)
                    if len(wsd) >= 2:
                        sc.wait("dve", f"dAF{job % 2}", wsd[-2][1])
                    ob = wst[job % 2]
                    sc.emit("dve", lambda e, b=bank, o=ob, f=fg:
                            e.scalar_tensor_tensor(
                                o[:, :], b[:, :], 1.0,
                                bo_sb[:, f * 512:(f + 1) * 512],
                                op0=ALU.mult, op1=ALU.add))
                    wev.append(sc.inc("dve", "cV"))
                    sc.wait("gp", "cV", wev[-1])
                    wsd.append((job % 2, sc.dma(
                        "gp", f"dAF{job % 2}",
                        af_s[job, :, fg * 512:(fg + 1) * 512], ob[:, :])))
                sc.barrier()
            al.reset(ph)

        if "pf" in phases:
            ph = al.scope()
            xnqh = al.tile([128, NDC, 512], BF16)
            xnql = al.tile([128, NDC, 512], BF16)
            acc = al.tile([128, NDC, 512])
            woff0 = al.scope()
            Wh0 = al.tile([128, 4, NDC, 128], BF16)
            Wl0 = al.tile([128, 4, NDC, 128], BF16)
            woff1 = al.scope()
            Wh1 = al.tile([128, 4, NDC, 128], BF16)
            Wl1 = al.tile([128, 4, NDC, 128], BF16)
            Wh_sb = [Wh0, Wh1]
            Wl_sb = [Wl0, Wl1]
            pr_sb = al.tile_at(woff0, [128, NDC, 512])
            ybuf4 = al.tile_at(woff1, [128, 4, D])
            EXBASE = al.scope()
            hb = [al.tile([128, 512]) for _ in range(2)]
            wbc = al.tile([128, 512])
            wcT8 = al.tile([128, 512])
            lg_fm = al.tile([128, 512])
            lg_tm = al.tile([128, 4, 8])
            mx8 = al.tile([128, 4, 8])
            mr1 = al.tile([128, 4, 8])
            mr2 = al.tile([128, 4, 8])
            row1 = al.tile([128, 4, 8])
            row2 = al.tile([128, 4, 8])
            e1m = al.tile([128, 4, 8])
            e2m = al.tile([128, 4, 8])
            wc = al.tile([128, 4, 8])
            wc2 = al.tile([128, 4, 8])
            w12 = al.tile([128, 4, 4])
            rtmp = al.tile([128, 4])
            wr_sb = al.tile([128, NDC, 8])
            wrh_sb = al.tile([128, NDC, 8], BF16)
            wrl_sb = al.tile([128, NDC, 8], BF16)
            pb_sb = al.tile([128, D])
            nw_sb = al.tile([128, D])
            xob = al.tile([128, D])
            afb = al.tile([128, D])
            ssq = al.tile([128, 4, 4])
            rsm = al.tile([128, 4])
            sdv = al.tile([128, 4])
            rrms = al.tile([128, 4])
            xn32j = al.tile([128, NDC, 128])

            sc.dma("sync", "dMISC", wr_sb[:], wr_r)
            sc.dma("sync", "dMISC", pb_sb[:, :], pb_b)
            misc_l = sc.dma("sync", "dMISC", nw_sb[:, :], nw_b)
            sc.wait("dve", "dMISC", misc_l)
            wrf = wr_sb.rearrange("p a b -> p (a b)")
            wrhf = wrh_sb.rearrange("p a b -> p (a b)")
            wrlf = wrl_sb.rearrange("p a b -> p (a b)")
            sc.emit("dve", lambda e: e.tensor_copy(wrhf, wrf))
            sc.rt("dve")
            sc.emit("dve", lambda e: e.tensor_tensor(wrlf, wrf, wrhf,
                                                     op=ALU.subtract))
            wr_sp = sc.inc("dve", "cV")

            prev_out = []
            for qr in range(4):
                jobs = [qr * 4 + jj for jj in range(4)]
                xl = 0
                for jj, job in enumerate(jobs):
                    for dc in range(NDC):
                        sc.dma("sync", "dXQ",
                               xnqh[:, dc, jj * 128:(jj + 1) * 128],
                               xnh_s[job, dc, :, :])
                        xl = sc.dma("sync", "dXQ",
                                    xnql[:, dc, jj * 128:(jj + 1) * 128],
                                    xnl_s[job, dc, :, :])
                bank = pbk[6]
                bank2 = pbk[7]
                sc.wait("pe", "dXQ", xl)
                if qr == 0:
                    sc.wait("pe", "cV", wr_sp)
                for d3 in range(3 * NDC):
                    dc, p3 = d3 // 3, d3 % 3
                    wr3 = (wrh_sb, wrh_sb, wrl_sb)[p3]
                    xn3 = (xnqh, xnql, xnqh)[p3]
                    sc.emit("pe", lambda e, d=dc, b=bank, w3=wr3, x3=xn3,
                            s0=(d3 == 0), s1=(d3 == 3 * NDC - 1):
                            e.matmul(b[:8, :], w3[:, d, :], x3[:, d, :],
                                     start=s0, stop=s1,
                                     skip_group_check=True))
                p1 = sc.inc("pe", "cP")
                sc.wait("act", "cP", p1)
                sc.emit("act", lambda e, b=bank:
                        e.activation(lg_fm[:8, :], b[:8, :], ACTF.Identity,
                                     bias=c_rb[:8, :]))
                a1 = sc.inc("act", "cA")
                sc.wait("pe", "cA", a1)
                for jj in range(4):
                    sc.emit("pe", lambda e, j2=jj, b=bank2:
                            e.transpose(b[:, j2 * 8:(j2 + 1) * 8],
                                        lg_fm[:8, j2 * 128:(j2 + 1) * 128],
                                        c_ident[:8, :8]))
                p2 = sc.inc("pe", "cP")
                sc.wait("dve", "cP", p2)
                for jj in range(4):
                    sc.emit("dve", lambda e, j2=jj:
                            e.tensor_copy(lg_tm[:, j2, :],
                                          bank2[:, j2 * 8:(j2 + 1) * 8]))
                sc.rt("dve")
                for jj in range(4):
                    sc.emit("dve", lambda e, j2=jj:
                            e.max(out=mx8[:, j2, :], in_=lg_tm[:, j2, :]))
                sc.rt("dve")
                for jj in range(4):
                    sc.emit("dve", lambda e, j2=jj:
                            e.memset(mr1[:, j2, :], NEG_BIG))
                    sc.emit("dve", lambda e, j2=jj:
                            e.memset(mr2[:, j2, :], NEG_BIG))
                sc.rt("dve")
                for jj in range(4):
                    sc.emit("dve", lambda e, j2=jj:
                            e.tensor_copy(mr1[:, j2, 0:1], mx8[:, j2, 0:1]))
                    sc.emit("dve", lambda e, j2=jj:
                            e.tensor_copy(mr2[:, j2, 0:1], mx8[:, j2, 1:2]))
                    sc.emit("dve", lambda e, j2=jj:
                            e.tensor_tensor(w12[:, j2, 0:1], mx8[:, j2, 1:2],
                                            mx8[:, j2, 0:1], op=ALU.subtract))
                v1 = sc.inc("dve", "cV")
                sc.wait("act", "cV", v1)
                for jj in range(4):
                    sc.emit("act", lambda e, j2=jj:
                            e.activation(w12[:, j2, 1:2], w12[:, j2, 0:1],
                                         ACTF.Exp))
                a2 = sc.inc("act", "cA")
                sc.wait("dve", "cA", a2)
                for jj in range(4):
                    sc.emit("dve", lambda e, j2=jj:
                            e.tensor_scalar(rtmp[:, j2:j2 + 1],
                                            w12[:, j2, 1:2], 1.0, None,
                                            op0=ALU.add))
                sc.rt("dve")
                for jj in range(4):
                    sc.emit("dve", lambda e, j2=jj:
                            e.reciprocal(w12[:, j2, 2:3], rtmp[:, j2:j2 + 1]))
                sc.rt("dve")
                for jj in range(4):
                    sc.emit("dve", lambda e, j2=jj:
                            e.tensor_scalar(w12[:, j2, 3:4], w12[:, j2, 2:3],
                                            1.0, -1.0, op0=ALU.subtract,
                                            op1=ALU.mult))
                sc.rt("dve")
                for jj in range(4):
                    sc.emit("dve", lambda e, j2=jj:
                            e.match_replace(out=row1[:, j2, :],
                                            in_to_replace=mr1[:, j2, :],
                                            in_values=lg_tm[:, j2, :],
                                            imm_value=-1.0e38))
                sc.rt("dve")
                for jj in range(4):
                    sc.emit("dve", lambda e, j2=jj:
                            e.tensor_tensor(e1m[:, j2, :], lg_tm[:, j2, :],
                                            row1[:, j2, :], op=ALU.not_equal))
                for jj in range(4):
                    sc.emit("dve", lambda e, j2=jj:
                            e.match_replace(out=row2[:, j2, :],
                                            in_to_replace=mr2[:, j2, :],
                                            in_values=row1[:, j2, :],
                                            imm_value=-1.0e38))
                sc.rt("dve")
                for jj in range(4):
                    sc.emit("dve", lambda e, j2=jj:
                            e.tensor_tensor(e2m[:, j2, :], row1[:, j2, :],
                                            row2[:, j2, :], op=ALU.not_equal))
                sc.rt("dve")
                for jj in range(4):
                    sc.emit("dve", lambda e, j2=jj:
                            e.tensor_tensor(wc[:, j2, :], e1m[:, j2, :],
                                            w12[:, j2, 2:3].to_broadcast(
                                                [128, 8]), op=ALU.mult))
                for jj in range(4):
                    sc.emit("dve", lambda e, j2=jj:
                            e.tensor_tensor(wc2[:, j2, :], e2m[:, j2, :],
                                            w12[:, j2, 3:4].to_broadcast(
                                                [128, 8]), op=ALU.mult))
                sc.rt("dve")
                for jj in range(4):
                    sc.emit("dve", lambda e, j2=jj:
                            e.tensor_tensor(wc[:, j2, :], wc[:, j2, :],
                                            wc2[:, j2, :], op=ALU.add))
                wc_done = sc.inc("dve", "cV")
                sc.wait("pe", "cV", wc_done)
                for jj in range(4):
                    sc.emit("pe", lambda e, j2=jj, b=bank2:
                            e.transpose(b[:8, j2 * 128:(j2 + 1) * 128],
                                        wc[:, j2, :], c_ident[:, :]))
                p3 = sc.inc("pe", "cP")
                sc.wait("dve", "cP", p3)
                sc.emit("dve", lambda e, b=bank2:
                        e.tensor_copy(wcT8[:8, :], b[:8, :]))
                wct_done = sc.inc("dve", "cV")

                he_p1, he_act, cb_ev, cb_l = [], [], [], []
                acc_cb = {}
                wls = {}

                def pf_wload(e_, fcg_):
                    bi = (e_ * 4 + fcg_) % 2
                    if he_p1:
                        sc.wait("sync", "cP", he_p1[-1])
                    hflat = Wh_sb[bi].rearrange("p a b c -> p (a b c)")
                    lflat = Wl_sb[bi].rearrange("p a b c -> p (a b c)")
                    sc.dma("sync", f"dWE{bi}", hflat, ewh_s[e_, fcg_])
                    wls[(e_, fcg_)] = sc.dma("sync", f"dWE{bi}", lflat,
                                             ewl_s[e_, fcg_])

                pf_wload(0, 0)
                cb_last = 0
                for e_ in range(8):
                    bbank = pbk[6]
                    sc.wait("pe", "cV", wct_done)
                    if cb_ev:
                        sc.wait("pe", "cV", cb_ev[-1])
                    sc.emit("pe", lambda e, x=e_, b=bbank:
                            e.matmul(b[:, :], c_esel[0:8, x, :], wcT8[:8, :],
                                     start=True, stop=True,
                                     skip_group_check=True))
                    pb1 = sc.inc("pe", "cP")
                    sc.wait("dve", "cP", pb1)
                    if cb_l:
                        sc.wait("dve", "cV", cb_l[-1])
                    sc.emit("dve", lambda e, b=bbank:
                            e.tensor_copy(wbc[:, :], b[:, :]))
                    cb_ev.append(sc.inc("dve", "cV"))
                    for fcg in range(4):
                        bi = (e_ * 4 + fcg) % 2
                        nxt = e_ * 4 + fcg + 1
                        if nxt < 32:
                            pf_wload(nxt // 4, nxt % 4)
                        sc.wait("pe", f"dWE{(e_ * 4 + fcg) % 2}",
                                wls[(e_, fcg)])
                        for fi in range(4):
                            fc = fcg * 4 + fi
                            bank = pbk[fc % 2]
                            if len(he_act) >= 2:
                                sc.wait("pe", "cA", he_act[-2])
                            for d3 in range(3 * NDC):
                                dc, p3 = d3 // 3, d3 % 3
                                xn3 = (xnqh, xnql, xnqh)[p3]
                                w3 = (Wh_sb[bi], Wh_sb[bi], Wl_sb[bi])[p3]
                                sc.emit("pe", lambda e, d=dc, b=bank,
                                        w=w3, x3=xn3, f2=fi,
                                        s0=(d3 == 0),
                                        s1=(d3 == 3 * NDC - 1):
                                        e.matmul(b[:, :], w[:, f2, d, :],
                                                 x3[:, d, :],
                                                 start=s0, stop=s1,
                                                 skip_group_check=True))
                            he_p1.append(sc.inc("pe", "cP"))
                            sc.wait("act", "cP", he_p1[-1])
                            if len(cb_l) >= 2:
                                sc.wait("act", "cV", cb_l[-2])
                            hbb = hb[fc % 2]
                            sc.emit("act", lambda e, b=bank, h=hbb, x=e_, f=fc:
                                    e.activation(h[:, :], b[:, :], ACTF.Silu,
                                                 bias=c_ebpp[:, x, f:f + 1]))
                            he_act.append(sc.inc("act", "cA"))
                            sc.wait("dve", "cA", he_act[-1])
                            sc.wait("dve", "cV", cb_ev[-1])
                            if fc in acc_cb:
                                sc.wait("dve", "cV", acc_cb[fc])
                            if e_ == 0:
                                sc.emit("dve", lambda e, h=hbb, f=fc:
                                        e.tensor_tensor(acc[:, f, :], h[:, :],
                                                        wbc[:, :],
                                                        op=ALU.mult))
                            else:
                                sc.emit("dve", lambda e, h=hbb:
                                        e.tensor_tensor(h[:, :], h[:, :],
                                                        wbc[:, :],
                                                        op=ALU.mult))
                                sc.rt("dve")
                                sc.emit("dve", lambda e, h=hbb, f=fc:
                                        e.tensor_tensor(acc[:, f, :],
                                                        acc[:, f, :],
                                                        h[:, :], op=ALU.add))
                            cb_last = sc.inc("dve", "cV")
                            cb_l.append(cb_last)
                            acc_cb[fc] = cb_last

                last_he = he_p1[-1]
                sc.wait("pe", "cV", cb_last)
                sc.wait("pe", "dMISC", misc_l)
                ytr_hi = []
                xr_done = []
                sc.wait("dve", "dXQ", xl)
                for jj in range(4):
                    if xr_done:
                        sc.wait("dve", "cP", ytr_p1[-1])
                    sc.emit("dve", lambda e, j2=jj:
                            e.tensor_tensor(
                                xn32j[:, :, :],
                                xnqh[:, :, j2 * 128:(j2 + 1) * 128],
                                xnql[:, :, j2 * 128:(j2 + 1) * 128],
                                op=ALU.add))
                    xr = sc.inc("dve", "cV")
                    xr_done.append(xr)
                    sc.wait("pe", "cV", xr)
                    for half in range(4):
                        bank = pbk[2 + half % 2]
                        if len(ytr_hi) >= 2:
                            sc.wait("pe", "cA", ytr_hi[-2])
                        for q in range(4):
                            dc = half * 4 + q
                            sc.emit("pe", lambda e, d=dc, q2=q, b=bank:
                                    e.transpose(
                                        b[:, q2 * 128:(q2 + 1) * 128],
                                        xn32j[:, d, :],
                                        c_ident[:, :]))
                        ph1 = sc.inc("pe", "cP")
                        ytr_p1 = [ph1]
                        sc.wait("act", "cP", ph1)
                        for q in range(4):
                            dc = half * 4 + q
                            sc.emit("act", lambda e, d=dc, q2=q, b=bank,
                                    j2=jj:
                                    _copy(e,
                                          ybuf4[:, j2, d * 128:(d + 1) * 128],
                                          b[:, q2 * 128:(q2 + 1) * 128]))
                        ytr_hi.append(sc.inc("act", "cA"))
                ytr_last = ytr_hi[-1]
                sc.wait("dve", "cA", ytr_last)
                sc.wait("dve", "dMISC", misc_l)
                for jj in range(4):
                    sc.emit("dve", lambda e, j2=jj:
                            e.tensor_tensor(ybuf4[:, j2, :], ybuf4[:, j2, :],
                                            pb_sb[:, :], op=ALU.add))
                ypb_done = sc.inc("dve", "cV")
                prj_ev, prj_p1 = [], []
                for fg in range(4):
                    sc.wait("sync", "cP",
                            last_he if fg == 0 else prj_p1[-1])
                    prls = []
                    for fc in range(NDC):
                        prls.append(
                            sc.dma("sync", "dPR", pr_sb[:, fc, :],
                                   pr_r[fc, :, fg * 512:(fg + 1) * 512]))
                    prl = prls[-1]
                    for jj in range(4):
                        bank = pbk[4 + jj % 2]
                        if jj > 0:
                            sc.wait("pe", "dPR", prl)
                        if len(prj_ev) >= 2:
                            sc.wait("pe", "cV", prj_ev[-2])
                        for fc in range(NDC):
                            if jj == 0:
                                sc.wait("pe", "dPR", prls[fc])
                            sc.emit("pe", lambda e, f=fc, j2=jj, b=bank:
                                    e.matmul(b[:, :],
                                             acc[:, f, j2 * 128:(j2 + 1) * 128],
                                             pr_sb[:, f, :],
                                             start=(f == 0),
                                             stop=(f == NDC - 1)))
                        prj_p1.append(sc.inc("pe", "cP"))
                        sc.wait("dve", "cP", prj_p1[-1])
                        if fg == 0 and jj == 0:
                            sc.wait("dve", "cV", ypb_done)
                        sc.emit("dve", lambda e, j2=jj, b=bank, f=fg:
                                e.scalar_tensor_tensor(
                                    ybuf4[:, j2, f * 512:(f + 1) * 512],
                                    b[:, :], 1.0,
                                    ybuf4[:, j2, f * 512:(f + 1) * 512],
                                    op0=ALU.mult, op1=ALU.add))
                        prj_ev.append(sc.inc("dve", "cV"))
                sc.wait("act", "cV", prj_ev[-1])
                for jj in range(4):
                    for c4 in range(4):
                        sc.emit("act", lambda e, j2=jj, c2=c4:
                                e.activation(pbk[7][:, :],
                                             ybuf4[:, j2,
                                                   c2 * 512:(c2 + 1) * 512],
                                             ACTF.Square,
                                             accum_out=ssq[:, j2, c2:c2 + 1]))
                        sc.rt("act")
                aq = sc.inc("act", "cA")
                sc.wait("dve", "cA", aq)
                for jj in range(4):
                    sc.emit("dve", lambda e, j2=jj:
                            e.reduce_sum(rsm[:, j2:j2 + 1], ssq[:, j2, :],
                                         axis=AX))
                sc.rt("dve")
                for jj in range(4):
                    sc.emit("dve", lambda e, j2=jj:
                            e.tensor_scalar(rsm[:, j2:j2 + 1],
                                            rsm[:, j2:j2 + 1], 1.0 / D, None,
                                            op0=ALU.mult))
                v2 = sc.inc("dve", "cV")
                sc.wait("act", "cV", v2)
                for jj in range(4):
                    sc.emit("act", lambda e, j2=jj:
                            e.activation(sdv[:, j2:j2 + 1], rsm[:, j2:j2 + 1],
                                         ACTF.Sqrt, bias=eps_rms))
                a3 = sc.inc("act", "cA")
                sc.wait("dve", "cA", a3)
                for jj in range(4):
                    sc.emit("dve", lambda e, j2=jj:
                            e.reciprocal(rrms[:, j2:j2 + 1], sdv[:, j2:j2 + 1]))
                rr_done = sc.inc("dve", "cV")
                sc.wait("dve", "dMISC", misc_l)
                sc.wait("dve", "cV", rr_done)
                for jj in range(4):
                    job = qr * 4 + jj
                    if prev_out:
                        sc.wait("sync", "dO", prev_out[-1])
                    xl2 = sc.dma("sync", "dTL", xob[:, :], x_own[job])
                    al2 = sc.dma("sync", "dTL", afb[:, :], af_s[job])
                    sc.emit("dve", lambda e, j2=jj:
                            e.tensor_tensor(ybuf4[:, j2, :], ybuf4[:, j2, :],
                                            rrms[:, j2:j2 + 1].to_broadcast(
                                                [128, D]), op=ALU.mult))
                    sc.rt("dve")
                    sc.emit("dve", lambda e, j2=jj:
                            e.tensor_tensor(ybuf4[:, j2, :], ybuf4[:, j2, :],
                                            nw_sb[:, :], op=ALU.mult))
                    sc.wait("dve", "dTL", al2)
                    sc.emit("dve", lambda e:
                            e.tensor_tensor(xob[:, :], xob[:, :], afb[:, :],
                                            op=ALU.add))
                    sc.rt("dve")
                    sc.emit("dve", lambda e, j2=jj:
                            e.tensor_tensor(xob[:, :], xob[:, :],
                                            ybuf4[:, j2, :], op=ALU.add))
                    fo = sc.inc("dve", "cV")
                    sc.wait("gp", "cV", fo)
                    prev_out.append(
                        sc.dma("gp", "dO", out_own[job], xob[:, :]))
                sc.barrier()
            al.reset(ph)

        sc.wait("gp", "dO", sc.cnt["dO"])
        sc.barrier()

        with nc.Block() as block:
            @block.tensor
            def _(eng):
                for f in sc.ops["pe"]:
                    f(eng)

            @block.vector
            def _(eng):
                for f in sc.ops["dve"]:
                    f(eng)

            @block.scalar
            def _(eng):
                for f in sc.ops["act"]:
                    f(eng)

            @block.sync
            def _(eng):
                for f in sc.ops["sync"]:
                    f(eng)

            @block.gpsimd
            def _(eng):
                for f in sc.ops["gp"]:
                    f(eng)

    return nc


def make_inputs(x, wq, bq, wk, bk, wv, bv, wo, bo, router_w, router_b,
                expert_w, expert_b, proj_w, proj_b, norm_w):
    f = np.float32
    common = {}
    common["wq_r"] = np.ascontiguousarray(
        wq.reshape(D, NH, HD).transpose(1, 0, 2), f)
    common["wk_r"] = np.ascontiguousarray(
        wk.reshape(D, NKV, HD).transpose(1, 0, 2), f)
    common["wv_r"] = np.ascontiguousarray(
        wv.reshape(NDC, 128, NKV * HD).transpose(1, 0, 2), f)
    common["wo_r"] = np.ascontiguousarray(wo.reshape(NDC, 128, D), f)
    common["wr_r"] = np.ascontiguousarray(
        router_w.reshape(NDC, 128, 8).transpose(1, 0, 2), f)
    common["ew_r"] = np.ascontiguousarray(
        expert_w.reshape(8, NDC, 128, 4, 4, 128).transpose(0, 3, 2, 4, 1, 5), f)
    common["pr_r"] = np.ascontiguousarray(proj_w.reshape(NDC, 128, D), f)
    common["eb_pp"] = np.ascontiguousarray(
        expert_b.reshape(8, NDC, 128).transpose(2, 0, 1), f)
    es = np.zeros((128, 8, 128), f)
    for e in range(8):
        es[e, e, :] = 1.0
    common["esel"] = es
    common["bq_p"] = np.ascontiguousarray(bq.reshape(NH, 128).T, f)
    common["bk_p"] = np.ascontiguousarray(bk.reshape(NKV, 128).T, f)
    common["bv_b"] = np.ascontiguousarray(np.tile(bv[None, :], (128, 1)), f)
    common["bo_b"] = np.ascontiguousarray(np.tile(bo[None, :], (128, 1)), f)
    common["pb_b"] = np.ascontiguousarray(np.tile(proj_b[None, :], (128, 1)), f)
    common["nw_b"] = np.ascontiguousarray(np.tile(norm_w[None, :], (128, 1)), f)
    rb = np.zeros((128, 1), f)
    rb[:8, 0] = router_b
    common["rb_p"] = rb
    common["ident"] = np.eye(128, dtype=f)
    import ml_dtypes
    common["identb"] = np.eye(128).astype(ml_dtypes.bfloat16)
    cp = np.zeros((128, 8), f)
    cp[:, 0] = LN_EPS
    cp[:, 1] = RMS_EPS
    cp[:, 2] = NEG_BIG
    common["cpool"] = cp

    in_maps = []
    for core in range(8):
        b, r = core // 2, core % 2
        m = dict(common)
        m["x_full"] = np.ascontiguousarray(x[b], f)
        xo = np.empty((NJOB, 128, D), f)
        mk = np.zeros((NJOB, 128, 512), f)
        for job in range(NJOB):
            cls = job_class(r, job)
            xo[job] = x[b, cls * 128:(cls + 1) * 128, :]
            j = job // 2
            key = 512 * j + np.arange(512)[None, :]
            row = cls * 128 + np.arange(128)[:, None]
            mk[job] = np.where(key > row, MASK_ADD, 0.0).astype(f)
        m["x_own"] = xo
        m["masks"] = mk
        in_maps.append(m)
    return in_maps


_NC_CACHE = {}


def kernel(L0_attn_x, L0_attn_wq, L0_attn_bq, L0_attn_wk, L0_attn_bk,
           L0_attn_wv, L0_attn_bv, L0_attn_wo, L0_attn_bo,
           L0_ffn_router_w, L0_ffn_router_b, L0_ffn_expert_w,
           L0_ffn_expert_b, L0_ffn_proj_w, L0_ffn_proj_b, L0_ffn_norm_w):
    x = np.asarray(L0_attn_x, np.float32)
    in_maps = make_inputs(
        x, np.asarray(L0_attn_wq, np.float32), np.asarray(L0_attn_bq, np.float32),
        np.asarray(L0_attn_wk, np.float32), np.asarray(L0_attn_bk, np.float32),
        np.asarray(L0_attn_wv, np.float32), np.asarray(L0_attn_bv, np.float32),
        np.asarray(L0_attn_wo, np.float32), np.asarray(L0_attn_bo, np.float32),
        np.asarray(L0_ffn_router_w, np.float32),
        np.asarray(L0_ffn_router_b, np.float32),
        np.asarray(L0_ffn_expert_w, np.float32),
        np.asarray(L0_ffn_expert_b, np.float32),
        np.asarray(L0_ffn_proj_w, np.float32),
        np.asarray(L0_ffn_proj_b, np.float32),
        np.asarray(L0_ffn_norm_w, np.float32))
    if "nc" not in _NC_CACHE:
        _NC_CACHE["nc"] = build_program()
    nc = _NC_CACHE["nc"]
    res = run_bass_kernel_spmd(nc, in_maps, core_ids=list(range(8)))
    out = np.empty((B, S, D), np.float32)
    for core in range(8):
        b, r = core // 2, core % 2
        oo = np.asarray(res.results[core]["out_own"])
        for job in range(NJOB):
            cls = job_class(r, job)
            out[b, cls * 128:(cls + 1) * 128, :] = oo[job]
    return out


# version 11
# speedup vs baseline: 1.1057x; 1.1057x over previous
import numpy as np
from contextlib import ExitStack

import concourse.bass as bass
import concourse.mybir as mybir
from concourse.bass_utils import run_bass_kernel_spmd

F32 = mybir.dt.float32
BF16 = mybir.dt.bfloat16
ALU = mybir.AluOpType
ACTF = mybir.ActivationFunctionType
AX = mybir.AxisListType.X

B, S, D = 4, 4096, 2048
NH, NKV, HD = 16, 4, 128
NJOB = 16
NDC = D // 128
SM_SCALE = float(np.float32(1.0 / np.sqrt(128.0)))
LN_EPS = 1e-5
LN_MODE = 2
N_LN_TILES = 32
RMS_EPS = 1e-6
NEG_BIG = -3.0e38
MASK_ADD = -1.0e9


def job_class(role, job):
    j, s = job // 2, job % 2
    return 4 * j + ((3 if s else 0) if role == 0 else (2 if s else 1))


def nkc_of(job):
    return job // 2 + 1


def nkt_of(job):
    j, s = job // 2, job % 2
    return 4 * j + (4 if s else 2)


def _copy(e, out, in_):
    if hasattr(e, "tensor_copy"):
        return e.tensor_copy(out, in_)
    return e.copy(out, in_)


class Sched:
    ENGINES = ("pe", "dve", "act", "sync", "gp")

    def __init__(self, nc):
        self.nc = nc
        self.ops = {e: [] for e in self.ENGINES}
        self.kinds = {e: [] for e in self.ENGINES}
        self.cnt = {}
        self.sem = {}

    def new_sem(self, ctx, name):
        self.sem[name] = ctx.enter_context(self.nc.semaphore(name))
        self.cnt[name] = 0

    def emit(self, eng, fn, kind="op"):
        self.ops[eng].append(fn)
        self.kinds[eng].append(kind)

    def inc(self, eng, name, by=1):
        sem = self.sem[name]
        self.cnt[name] += by
        if self.kinds[eng] and self.kinds[eng][-1] == "op":
            last = self.ops[eng].pop()
            self.kinds[eng].pop()
            self.emit(eng, lambda e, f=last: f(e).then_inc(sem, by),
                      kind="op+inc")
        else:
            self.emit(eng, lambda e: e.drain().then_inc(sem, by),
                      kind="op+inc")
        return self.cnt[name]

    def wait(self, eng, name, value=None):
        if value is None:
            value = self.cnt[name]
        if value <= 0:
            return
        assert value <= self.cnt[name], (name, value, self.cnt[name])
        sem = self.sem[name]
        self.emit(eng, lambda e: e.wait_ge(sem, value), kind="wait")

    def dma(self, eng, name, out, in_):
        sem = self.sem[name]
        self.cnt[name] += 16
        self.emit(eng,
                  lambda e: e.dma_start(out=out, in_=in_).then_inc(sem, 16),
                  kind="op+inc")
        return self.cnt[name]

    def rt(self, eng):
        v = self.inc(eng, "rt_" + eng)
        self.wait(eng, "rt_" + eng, v)

    def barrier(self):
        dma_sems = [n for n in self.cnt if n.startswith("d")]
        for e in self.ENGINES:
            if e in ("sync", "gp"):
                for s_ in dma_sems:
                    self.wait(e, s_, self.cnt[s_])
            self.emit(e, lambda en, s=self.sem["b_" + e]:
                      en.drain().then_inc(s, 1), kind="op+inc")
            self.cnt["b_" + e] += 1
        tgt = {e: self.cnt["b_" + e] for e in self.ENGINES}
        for e in self.ENGINES:
            for o in self.ENGINES:
                if o != e:
                    self.wait(e, "b_" + o, tgt[o])


class SbufAlloc:
    def __init__(self, nc, base, limit):
        self.nc = nc
        self.limit = limit
        self.off = base
        self.n = 0
        self.offs = {}

    def tile(self, shape, dtype=F32):
        per_part = int(np.prod(shape[1:])) * mybir.dt.size(dtype)
        per_part = (per_part + 31) // 32 * 32
        ap = self.tile_at(self.off, shape, dtype)
        self.off += per_part
        return ap

    def tile_at(self, off, shape, dtype=F32):
        per_part = int(np.prod(shape[1:])) * mybir.dt.size(dtype)
        assert off + per_part <= self.limit, (
            f"SBUF over budget: {off}+{per_part} > {self.limit}")
        h = self.nc.alloc_sbuf_tensor_at(
            f"sb{self.n}", list(shape), dtype, offset=off)
        ap = h.ap()
        self.offs[id(ap)] = off
        self.n += 1
        return ap

    def off_of(self, ap):
        return self.offs[id(ap)]

    def scope(self):
        return self.off

    def reset(self, off):
        self.off = off


def build_program(debug=False,
                  phases=("ps", "p0", "p0b", "pq", "pa", "pw", "pf")):
    nc = bass.Bass("TRN2", target_bir_lowering=False)

    def din(name, shape):
        return nc.dram_tensor(name, list(shape), F32, kind="ExternalInput").ap()

    x_full = din("x_full", (S, D))
    x_own = din("x_own", (NJOB, 128, D))
    masks = din("masks", (NJOB, 128, 512))
    wq_r = din("wq_r", (NH, D, HD))
    wk_r = din("wk_r", (NKV, D, HD))
    wv_r = din("wv_r", (128, NDC, NKV * HD))
    wo_r = din("wo_r", (NDC, 128, D))
    wr_r = din("wr_r", (128, NDC, 8))
    ew_r = din("ew_r", (8, 4, 128, 4, NDC, 128))
    pr_r = din("pr_r", (NDC, 128, D))
    eb_pp = din("eb_pp", (128, 8, NDC))
    esel = din("esel", (128, 8, 128))
    bq_p = din("bq_p", (128, NH))
    bk_p = din("bk_p", (128, NKV))
    bv_b = din("bv_b", (128, NKV * HD))
    bo_b = din("bo_b", (128, D))
    pb_b = din("pb_b", (128, D))
    nw_b = din("nw_b", (128, D))
    rb_p = din("rb_p", (128, 1))
    ident_in = din("ident", (128, 128))
    identb_in = nc.dram_tensor("identb", [128, 128], BF16,
                               kind="ExternalInput").ap()
    cpool_in = din("cpool", (128, 8))

    out_own = nc.dram_tensor("out_own", [NJOB, 128, D], F32,
                             kind="ExternalOutput").ap()

    skind = "ExternalOutput" if debug else "Internal"

    def scr(name, shape):
        return nc.dram_tensor(name, list(shape), F32, kind=skind).ap()

    def scrb(name, shape):
        return nc.dram_tensor(name, list(shape), BF16, kind=skind).ap()

    ewh_s = scrb("ewh_s", (8, 4, 128, 4 * NDC * 128))
    ewl_s = scrb("ewl_s", (8, 4, 128, 4 * NDC * 128))
    xnh_s = scrb("xnh_s", (NJOB, NDC, 128, 128))
    xnl_s = scrb("xnl_s", (NJOB, NDC, 128, 128))
    k_s = scr("k_s", (NKV, 128, S))
    v_s = scr("v_s", (S // 128, 128, NKV * HD))
    xnofm = scr("xnofm", (NJOB, NDC, 128, 128))
    q_s = scr("q_s", (NH, 128, 2048))
    at_s = scr("at_s", (NJOB, 128, D))
    af_s = scr("af_s", (NJOB, 128, D))

    with ExitStack() as ctx:
        sc = Sched(nc)
        semnames = (["dC", "dMISC", "dXQ", "dWkv", "dKVL", "dWO", "dPR",
                     "dTL", "dO", "cP", "cV", "cA",
                     "rt_dve", "rt_act", "rt_pe",
                     "b_pe", "b_dve", "b_act", "b_sync", "b_gp"]
                    + [f"dX{i}" for i in range(4)]
                    + [f"{p}{i}" for p in ("dKk", "dKv", "dXN", "dWQ", "dQS",
                                           "dQJ", "dAO", "dAT", "dAF", "dWE")
                       for i in range(2)])
        for s_ in semnames:
            sc.new_sem(ctx, s_)

        pbk = [nc.alloc_psum_tensor(f"ps{i}", [128, 512], F32).ap()
               for i in range(8)]

        al = SbufAlloc(nc, (nc._sbuf_addr_for_side('left') + 31) // 32 * 32,
               nc._sbuf_addr_for_side('right') - 64)

        c_ident = al.tile([128, 128])
        c_identb = al.tile([128, 128], BF16)
        c_esel = al.tile([128, 8, 128])
        c_pool = al.tile([128, 8])
        c_bq = al.tile([128, NH])
        c_bk = al.tile([128, NKV])
        c_rb = al.tile([128, 1])
        c_bvb = al.tile([128, NKV * HD])
        c_ebpp = al.tile([128, 8, NDC])
        c_zmask = al.tile([128, 512])
        PERSIST = al.scope()

        eps_ln = c_pool[:, 0:1]
        eps_rms = c_pool[:, 1:2]
        neg_init = c_pool[:, 2:3]

        for dst, src in [(c_ident, ident_in), (c_identb, identb_in),
                         (c_esel, esel),
                         (c_pool, cpool_in), (c_bq, bq_p), (c_bk, bk_p),
                         (c_rb, rb_p), (c_bvb, bv_b), (c_ebpp, eb_pp)]:
            sc.dma("sync", "dC", dst[:], src)
        sc.wait("dve", "dC")
        sc.emit("dve", lambda e: e.memset(c_zmask[:, :], 0.0))
        sc.barrier()

        def transpose_2048(src_col_fn, dst_fn, banks, evac_eng, state):
            ev = state.setdefault("ev", [])
            nm = "cA" if evac_eng == "act" else "cV"
            for half in range(4):
                bank = banks[half % 2]
                if len(ev) >= 2:
                    sc.wait("pe", nm, ev[-2])
                for q in range(4):
                    dc = half * 4 + q
                    sc.emit("pe", lambda e, d=dc, q2=q, b=bank:
                            e.transpose(b[:, q2 * 128:(q2 + 1) * 128],
                                        src_col_fn(d), c_ident[:, :]))
                p1 = sc.inc("pe", "cP")
                sc.wait(evac_eng, "cP", p1)
                for q in range(4):
                    dc = half * 4 + q
                    sc.emit(evac_eng, lambda e, d=dc, q2=q, b=bank:
                            _copy(e, dst_fn(d),
                                  b[:, q2 * 128:(q2 + 1) * 128]))
                ev.append(sc.inc(evac_eng, nm))
            return ev[-1]

        sp_state = {"v": [], "st": [], "ld": []}
        sp_bufs = {}

        def make_split_feeders():
            w32 = sp_bufs["w32"]
            whb = sp_bufs["whb"]
            wlb = sp_bufs["wlb"]

            def chunk(ci):
                e_, fcg = ci // 4, ci % 4
                if sp_state["v"]:
                    sc.wait("sync", "cV", sp_state["v"][-1])
                wl = sc.dma("sync", "dWE0",
                            w32[:, :],
                            ew_r[e_, fcg].rearrange("p a b c -> p (a b c)"))
                sc.wait("dve", "dWE0", wl)
                if sp_state["st"]:
                    sc.wait("dve", "dXN0", sp_state["st"][-1])
                sc.emit("dve", lambda e:
                        e.tensor_copy(whb[:, :], w32[:, :]))
                sc.rt("dve")
                sc.emit("dve", lambda e:
                        e.tensor_tensor(wlb[:, :], w32[:, :],
                                        whb[:, :], op=ALU.subtract))
                sp_state["v"].append(sc.inc("dve", "cV"))
                sc.wait("gp", "cV", sp_state["v"][-1])
                sc.dma("gp", "dXN0", ewh_s[e_, fcg], whb[:, :])
                sp_state["st"].append(sc.dma("gp", "dXN0", ewl_s[e_, fcg],
                                             wlb[:, :]))

            return [(lambda ci=ci: chunk(ci)) for ci in range(32)]

        def emit_ln(src_fn, ntiles, xnfm_g, gcols, sink_fn, sink_every,
                    pre_evac_wait=None):
            ph = al.scope()
            xbuf = [al.tile([128, D]) for _ in range(4)]
            sqj = al.tile([128, D])
            xnb = [al.tile([128, D]) for _ in range(2)]
            st = {n: [al.tile([128, 1]) for _ in range(4)]
                  for n in ("sx", "sq", "m", "va", "sd", "rstd", "nmr", "tmp")}
            apply_done = {}
            tr_done = {}
            trst = {}
            self_sq = [0]

            def stats(t, ld_val):
                i = t % 4
                sx, sq_, m_, va, sd_, rstd, nmr, tmp = (
                    st[n][i] for n in ("sx", "sq", "m", "va", "sd",
                                       "rstd", "nmr", "tmp"))
                sc.wait("dve", f"dX{i}", ld_val)
                sc.emit("dve", lambda e: e.reduce_sum(sx, xbuf[i][:, :], axis=AX))
                sc.rt("dve")
                sc.wait("act", f"dX{i}", ld_val)
                sc.wait("act", "cA", self_sq[0])
                sc.emit("act", lambda e: e.activation(
                    sqj[:, :], xbuf[i][:, :], ACTF.Square, accum_out=sq_))
                a1 = sc.inc("act", "cA")
                self_sq[0] = a1
                sc.wait("dve", "cA", a1)
                sc.emit("dve", lambda e: e.tensor_scalar(
                    m_, sx, 1.0 / D, None, op0=ALU.mult))
                sc.rt("dve")
                sc.emit("dve", lambda e: e.tensor_tensor(tmp, m_, m_, op=ALU.mult))
                sc.emit("dve", lambda e: e.tensor_scalar(
                    va, sq_, 1.0 / D, None, op0=ALU.mult))
                sc.rt("dve")
                sc.emit("dve", lambda e: e.tensor_tensor(
                    va, va, tmp, op=ALU.subtract))
                v1 = sc.inc("dve", "cV")
                sc.wait("act", "cV", v1)
                sc.emit("act", lambda e: e.activation(
                    sd_, va, ACTF.Sqrt, bias=eps_ln))
                a2 = sc.inc("act", "cA")
                sc.wait("dve", "cA", a2)
                sc.emit("dve", lambda e: e.reciprocal(rstd, sd_))
                sc.rt("dve")
                sc.emit("dve", lambda e: e.tensor_tensor(
                    nmr, m_, rstd, op=ALU.mult))
                sc.rt("dve")
                sc.emit("dve", lambda e: e.tensor_scalar(
                    nmr, nmr, -1.0, None, op0=ALU.mult))
                return sc.inc("dve", "cV")

            def apply_tr(t, stats_ready):
                i = t % 4
                rstd, nmr = st["rstd"][i], st["nmr"][i]
                sc.wait("act", "cV", stats_ready)
                if LN_MODE >= 1:
                    sc.emit("act", lambda e: e.activation(
                        xnb[t % 2][:, :], xbuf[i][:, :], ACTF.Identity,
                        bias=nmr, scale=rstd))
                else:
                    sc.emit("act", lambda e: _copy(
                        e, xnb[t % 2][:, :], xbuf[i][:, :]))
                apply_done[t] = sc.inc("act", "cA")
                if pre_evac_wait is not None:
                    pre_evac_wait(t)
                if LN_MODE == 1 or LN_MODE == 0:
                    tr_done[t] = apply_done[t]
                    return
                sc.wait("pe", "cA", apply_done[t])
                col = t % gcols
                tr_done[t] = transpose_2048(
                    lambda dc: xnb[t % 2][:, dc * 128:(dc + 1) * 128],
                    lambda dc: xnfm_g[:, dc, col * 128:(col + 1) * 128],
                    (pbk[0], pbk[1]), "act", trst)

            pend = []
            for t in range(ntiles + 2):
                if t < ntiles:
                    if t >= 4:
                        sc.wait("sync", "cA", apply_done[t - 4])
                    ld = sc.dma("sync", f"dX{t % 4}", xbuf[t % 4][:, :],
                                src_fn(t))
                    pend.append((t, stats(t, ld)))
                if len(pend) > 2 or (t >= ntiles and pend):
                    pt, pr = pend.pop(0)
                    apply_tr(pt, pr)
                    if (pt + 1) % sink_every == 0:
                        sink_fn(pt // sink_every, tr_done[pt])
            al.reset(ph)

        if "p0" in phases or "p0ln" in phases:
            ph0 = al.scope()
            xnfm_g = al.tile([128, NDC, 512])
            wk_sb = al.tile([128, NKV * NDC, HD])
            wv_sb = al.tile([128, NDC, NKV * HD])
            kst = [al.tile([128, 512]) for _ in range(2)]
            vst = [al.tile([128, 512]) for _ in range(2)]

            for fc in range(NKV):
                for dc in range(NDC):
                    sc.dma("sync", "dWkv", wk_sb[:, fc * NDC + dc, :],
                           wk_r[fc, dc * 128:(dc + 1) * 128, :])
            sc.dma("sync", "dWkv", wv_sb[:, :, :], wv_r)
            w_ld = sc.cnt["dWkv"]

            kv_ev, v_ev, k_st, v_st = [], [], [], []

            def kv_sink(g, tr_ready):
                sc.wait("pe", "cA", tr_ready)
                if g == 0:
                    sc.wait("pe", "dWkv", w_ld)
                for fc in range(NKV):
                    bank = pbk[2 + fc % 2]
                    if len(kv_ev) >= 2:
                        sc.wait("pe", "cA", kv_ev[-2])
                    for dc in range(NDC):
                        sc.emit("pe", lambda e, f=fc, d=dc, b=bank:
                                e.matmul(b[:, :], wk_sb[:, f * NDC + d, :],
                                         xnfm_g[:, d, :],
                                         start=(d == 0), stop=(d == NDC - 1)))
                    p1 = sc.inc("pe", "cP")
                    sc.wait("act", "cP", p1)
                    if len(k_st) >= 2:
                        sc.wait("act", f"dKk{fc % 2}", k_st[-2][1])
                    kb = kst[fc % 2]
                    sc.emit("act", lambda e, b=bank, k=kb, f=fc:
                            e.activation(k[:, :], b[:, :], ACTF.Identity,
                                         bias=c_bk[:, f:f + 1]))
                    kv_ev.append(sc.inc("act", "cA"))
                    sc.wait("gp", "cA", kv_ev[-1])
                    k_st.append((fc % 2, sc.dma(
                        "gp", f"dKk{fc % 2}",
                        k_s[fc, :, g * 512:(g + 1) * 512], kb[:, :])))
                for kt in range(4):
                    bank = pbk[4 + kt % 2]
                    if len(v_ev) >= 2:
                        sc.wait("pe", "cV", v_ev[-2])
                    for dc in range(NDC):
                        sc.emit("pe", lambda e, k=kt, d=dc, b=bank:
                                e.matmul(b[:, :],
                                         xnfm_g[:, d, k * 128:(k + 1) * 128],
                                         wv_sb[:, d, :],
                                         start=(d == 0), stop=(d == NDC - 1)))
                    p1 = sc.inc("pe", "cP")
                    sc.wait("dve", "cP", p1)
                    if len(v_st) >= 2:
                        sc.wait("dve", f"dKv{kt % 2}", v_st[-2][1])
                    vb = vst[kt % 2]
                    sc.emit("dve", lambda e, b=bank, v=vb:
                            e.scalar_tensor_tensor(v[:, :], b[:, :], 1.0,
                                                   c_bvb[:, :],
                                                   op0=ALU.mult, op1=ALU.add))
                    v_ev.append(sc.inc("dve", "cV"))
                    sc.wait("gp", "cV", v_ev[-1])
                    v_st.append((kt % 2, sc.dma(
                        "gp", f"dKv{kt % 2}", v_s[g * 4 + kt, :, :], vb[:, :])))

            if "p0ln" in phases:
                emit_ln(lambda t: x_full[t * 128:(t + 1) * 128, :],
                        N_LN_TILES, xnfm_g, 4, lambda g, tr: None, 4)
            else:
                emit_ln(lambda t: x_full[t * 128:(t + 1) * 128, :],
                        32, xnfm_g, 4, kv_sink, 4)
            al.reset(ph0)
            sc.barrier()

        if "p0b" in phases:
            ph = al.scope()
            xnfm_1 = al.tile([128, NDC, 128])
            xnfm1h = al.tile([128, NDC, 128], BF16)
            xnfm1l = al.tile([128, NDC, 128], BF16)
            own_st = {}
            own_sp = {}

            def own_pre_evac(t):
                if t - 1 in own_st:
                    sc.wait("act", f"dXN{(t - 1) % 2}", own_st[t - 1])
                if t - 1 in own_sp:
                    sc.wait("act", "cV", own_sp[t - 1])

            def own_sink(t, tr_ready):
                sc.wait("dve", "cA", tr_ready)
                if t - 1 in own_st:
                    sc.wait("dve", f"dXN{(t - 1) % 2}", own_st[t - 1])
                fm_flat = xnfm_1.rearrange("p a b -> p (a b)")
                h_flat = xnfm1h.rearrange("p a b -> p (a b)")
                l_flat = xnfm1l.rearrange("p a b -> p (a b)")
                sc.emit("dve", lambda e: e.tensor_copy(h_flat, fm_flat))
                sc.rt("dve")
                sc.emit("dve", lambda e:
                        e.tensor_tensor(l_flat, fm_flat, h_flat,
                                        op=ALU.subtract))
                own_sp[t] = sc.inc("dve", "cV")
                sc.wait("gp", "cA", tr_ready)
                sc.wait("gp", "cV", own_sp[t])
                for dc in range(NDC):
                    sc.dma("gp", f"dXN{t % 2}", xnofm[t, dc, :, :],
                           xnfm_1[:, dc, :])
                    sc.dma("gp", f"dXN{t % 2}", xnh_s[t, dc, :, :],
                           xnfm1h[:, dc, :])
                    sc.dma("gp", f"dXN{t % 2}", xnl_s[t, dc, :, :],
                           xnfm1l[:, dc, :])
                own_st[t] = sc.cnt[f"dXN{t % 2}"]

            emit_ln(lambda t: x_own[t], NJOB, xnfm_1, 1, own_sink, 1,
                    pre_evac_wait=own_pre_evac)
            al.reset(ph)
            sc.barrier()

        if "pq" in phases:
            ph = al.scope()
            xnq = al.tile([128, NDC, 512])
            wq_sb = [al.tile([128, NDC, HD]) for _ in range(2)]
            qst = [al.tile([128, 512]) for _ in range(2)]
            sp_bufs["w32"] = al.tile([128, 8192])
            sp_bufs["whb"] = al.tile([128, 8192], BF16)
            sp_bufs["wlb"] = al.tile([128, 8192], BF16)
            feeders = make_split_feeders() if "ps" in phases else []
            qev, qsd, qp1 = [], [], []
            for qg in range(4):
                if qp1:
                    sc.wait("sync", "cP", qp1[-1])
                lds = 0
                for jj in range(4):
                    job = qg * 4 + jj
                    for dc in range(NDC):
                        lds = sc.dma("sync", "dXQ",
                                     xnq[:, dc, jj * 128:(jj + 1) * 128],
                                     xnofm[job, dc, :, :])
                for fc in range(NH):
                    fci = qg * NH + fc
                    if fci % 2 == 0 and feeders:
                        feeders.pop(0)()
                    wb = wq_sb[fci % 2]
                    if len(qp1) >= 2:
                        sc.wait("sync", "cP", qp1[-2])
                    wl = 0
                    for dc in range(NDC):
                        wl = sc.dma("sync", f"dWQ{fci % 2}", wb[:, dc, :],
                                    wq_r[fc, dc * 128:(dc + 1) * 128, :])
                    bank = pbk[fci % 2]
                    sc.wait("pe", f"dWQ{fci % 2}", wl)
                    sc.wait("pe", "dXQ", lds)
                    if len(qev) >= 2:
                        sc.wait("pe", "cA", qev[-2])
                    for dc in range(NDC):
                        sc.emit("pe", lambda e, d=dc, b=bank, w=wb:
                                e.matmul(b[:, :], w[:, d, :], xnq[:, d, :],
                                         start=(d == 0), stop=(d == NDC - 1)))
                    qp1.append(sc.inc("pe", "cP"))
                    sc.wait("act", "cP", qp1[-1])
                    if len(qsd) >= 2:
                        sc.wait("act", f"dQS{fci % 2}", qsd[-2][1])
                    qb = qst[fci % 2]
                    sc.emit("act", lambda e, b=bank, q=qb, f=fc:
                            e.activation(q[:, :], b[:, :], ACTF.Identity,
                                         bias=c_bq[:, f:f + 1]))
                    qev.append(sc.inc("act", "cA"))
                    sc.wait("gp", "cA", qev[-1])
                    qsd.append((fci % 2, sc.dma(
                        "gp", f"dQS{fci % 2}",
                        q_s[fc, :, qg * 512:(qg + 1) * 512], qb[:, :])))
            al.reset(ph)
            sc.barrier()

        if "pa" in phases:
            ph = al.scope()
            K_h = al.tile([128, S])
            V_h = al.tile([128, S // 128, HD])
            qjob = [al.tile([128, 4, HD]) for _ in range(2)]
            mbuf = [al.tile([128, 512]) for _ in range(2)]
            P_b = [al.tile([128, 4, S]) for _ in range(2)]
            mx = al.tile([128, 4, 8])
            negmax = [al.tile([128, 4]) for _ in range(2)]
            sumex = [al.tile([128, 4]) for _ in range(2)]
            rs_t = [al.tile([128, 4]) for _ in range(2)]
            pt_sb = [al.tile([128, 512]) for _ in range(2)]
            ast = [al.tile([128, 512]) for _ in range(2)]
            rsT_sb = [al.tile([128, 128]) for _ in range(2)]
            rsb_sb = [al.tile([128, 512]) for _ in range(2)]
            jnk_a = al.tile([128, 512])

            for kvh in range(NKV):
                kl = sc.dma("sync", "dKVL", K_h[:, :], k_s[kvh])
                for kt in range(S // 128):
                    kl = sc.dma("sync", "dKVL", V_h[:, kt, :],
                                v_s[kt, :, kvh * HD:(kvh + 1) * HD])
                q_lds, m_lds = {}, {}
                sv_ev, exp_done, rs_done = {}, {}, {}
                av_done, ao_ev, ao_st, sc_p1, rs_ev = {}, {}, {}, {}, {}

                def load_job(job):
                    i = job % 2
                    if job >= 2:
                        sc.wait("sync", "cP", sc_p1[job - 2])
                        sc.wait("sync", "cV", sv_ev[job - 2])
                    ql = 0
                    for h in range(4):
                        ql = sc.dma("sync", f"dQJ{i}", qjob[i][:, h, :],
                                    q_s[kvh * 4 + h, :,
                                        job * 128:(job + 1) * 128])
                    q_lds[job] = ql
                    m_lds[job] = sc.dma("sync", f"dQJ{i}", mbuf[i][:, :],
                                        masks[job])

                def emit_scores(job):
                    i = job % 2
                    nkc = nkc_of(job)
                    sc.wait("pe", f"dQJ{job % 2}", m_lds[job])
                    if job == 0:
                        sc.wait("pe", "dKVL", kl)
                    if job >= 1:
                        sc.wait("pe", "cV", sv_ev[job - 1])
                    if job >= 2:
                        sc.wait("pe", "cP", av_done[job - 2])
                    evs = []
                    for h in range(4):
                        for kc in range(nkc):
                            bank = pbk[(h * nkc + kc) % 2]
                            if len(evs) >= 2:
                                sc.wait("pe", "cV", evs[-2])
                            sc.emit("pe", lambda e, h2=h, k2=kc, b=bank, i2=i:
                                    e.matmul(b[:, :], qjob[i2][:, h2, :],
                                             K_h[:, k2 * 512:(k2 + 1) * 512],
                                             start=True, stop=True,
                                             skip_group_check=True))
                            p1 = sc.inc("pe", "cP")
                            sc.wait("dve", "cP", p1)
                            msk = mbuf[i] if kc == nkc - 1 else c_zmask
                            sc.emit("dve", lambda e, h2=h, k2=kc, b=bank,
                                    m2=msk, i2=i:
                                    e.scalar_tensor_tensor(
                                        P_b[i2][:, h2, k2 * 512:(k2 + 1) * 512],
                                        b[:, :], SM_SCALE, m2[:, :],
                                        op0=ALU.mult, op1=ALU.add))
                            evs.append(sc.inc("dve", "cV"))
                    sc.rt("dve")
                    for h in range(4):
                        for kc in range(nkc):
                            sc.emit("dve", lambda e, h2=h, k2=kc, i2=i:
                                    e.reduce_max(
                                        mx[:, h2, k2:k2 + 1],
                                        P_b[i2][:, h2, k2 * 512:(k2 + 1) * 512],
                                        axis=AX))
                    sc.rt("dve")
                    for h in range(4):
                        sc.emit("dve", lambda e, h2=h, n2=nkc, i2=i:
                                e.reduce_max(negmax[i2][:, h2:h2 + 1],
                                             mx[:, h2, 0:n2], axis=AX,
                                             negate=True))
                    sc_p1[job] = sc.cnt["cP"]
                    sv_ev[job] = sc.inc("dve", "cV")

                def emit_exps(job):
                    i = job % 2
                    nkc = nkc_of(job)
                    sc.wait("act", "cV", sv_ev[job])
                    if job <= 1:
                        for q_ in range(4):
                            sc.emit("act", lambda e, q2=q_: _copy(
                                e, jnk_a[:, q2 * 128:(q2 + 1) * 128],
                                c_zmask[:, 0:128]))
                    for h in range(4):
                        sc.emit("act", lambda e, h2=h, i2=i, n2=nkc:
                                e.activation(P_b[i2][:, h2, 0:n2 * 512],
                                             P_b[i2][:, h2, 0:n2 * 512],
                                             ACTF.Exp,
                                             bias=negmax[i2][:, h2:h2 + 1],
                                             accum_out=sumex[i2][:, h2:h2 + 1]))
                    exp_done[job] = sc.inc("act", "cA")
                    sc.wait("dve", "cA", exp_done[job])
                    sc.emit("dve", lambda e, i2=i:
                            e.reciprocal(rs_t[i2][:, :], sumex[i2][:, :]))
                    rs_done[job] = sc.inc("dve", "cV")

                def emit_av(job):
                    i = job % 2
                    nkt = nkt_of(job)
                    obank = pbk[4 + job % 2]
                    sc.wait("pe", "cA", exp_done[job])
                    if job >= 2:
                        sc.wait("pe", "cV", ao_ev[job - 2])
                    ptev = []

                    def av_tr(kt):
                        tbank = pbk[2 + kt % 2]
                        if kt >= 2:
                            sc.wait("pe", "cA", ptev[kt - 2])
                        for h in range(4):
                            sc.emit("pe", lambda e, h2=h, k2=kt, b=tbank,
                                    i2=i:
                                    e.transpose(
                                        b[:, h2 * 128:(h2 + 1) * 128],
                                        P_b[i2][:, h2,
                                                k2 * 128:(k2 + 1) * 128],
                                        c_ident[:, :]))
                        p1 = sc.inc("pe", "cP")
                        sc.wait("act", "cP", p1)
                        pt = pt_sb[kt % 2]
                        sc.emit("act", lambda e, b=tbank, p2=pt:
                                _copy(e, p2[:, :], b[:, :]))
                        ptev.append(sc.inc("act", "cA"))

                    av_tr(0)
                    for kt in range(nkt):
                        if kt + 1 < nkt:
                            av_tr(kt + 1)
                        pt = pt_sb[kt % 2]
                        sc.wait("pe", "cA", ptev[kt])
                        sc.emit("pe", lambda e, k2=kt, p2=pt, b=obank, n2=nkt:
                                e.matmul(b[:, :], V_h[:, k2, :], p2[:, :],
                                         start=(k2 == 0),
                                         stop=(k2 == n2 - 1),
                                         skip_group_check=True))
                    av_done[job] = sc.inc("pe", "cP")
                    sc.wait("pe", "cV", rs_done[job])
                    if job >= 1 and (job - 1) in rs_ev:
                        sc.wait("pe", "cV", rs_ev[job - 1])
                    sc.emit("pe", lambda e, i2=i:
                            e.transpose(pbk[6][:4, 0:128], rs_t[i2][:, :],
                                        c_ident[:, :]))
                    pt1 = sc.inc("pe", "cP")
                    sc.wait("dve", "cP", pt1)
                    sc.emit("dve", lambda e, i2=i:
                            _copy(e, rsT_sb[i2][:4, :], pbk[6][:4, 0:128]))
                    rtv = sc.inc("dve", "cV")
                    sc.wait("pe", "cV", rtv)
                    for h in range(4):
                        sc.emit("pe", lambda e, h2=h, i2=i:
                                e.matmul(pbk[7][:, h2 * 128:(h2 + 1) * 128],
                                         c_esel[0:4, h2, :],
                                         rsT_sb[i2][:4, :],
                                         start=(h2 == 0), stop=(h2 == 3),
                                         skip_group_check=True))
                    pb2 = sc.inc("pe", "cP")
                    sc.wait("dve", "cP", pb2)
                    sc.emit("dve", lambda e, i2=i:
                            _copy(e, rsb_sb[i2][:, :], pbk[7][:, :]))
                    rbv = sc.inc("dve", "cV")
                    rs_ev[job] = rbv
                    sc.wait("dve", "cP", av_done[job])
                    sc.wait("dve", "cV", rbv)
                    if job >= 2:
                        sc.wait("dve", f"dAO{job % 2}", ao_st[job - 2])
                    ab = ast[job % 2]
                    sc.emit("dve", lambda e, b=obank, a=ab, i2=i:
                            e.tensor_tensor(a[:, :], b[:, :],
                                            rsb_sb[i2][:, :], op=ALU.mult))
                    ao_ev[job] = sc.inc("dve", "cV")
                    sc.wait("gp", "cV", ao_ev[job])
                    ao_st[job] = sc.dma(
                        "gp", f"dAO{job % 2}",
                        at_s[job, :, kvh * 512:(kvh + 1) * 512], ab[:, :])

                load_job(0)
                for job in range(NJOB):
                    emit_scores(job)
                    if job + 1 < NJOB:
                        load_job(job + 1)
                    if job >= 1:
                        emit_av(job - 1)
                    emit_exps(job)
                emit_av(NJOB - 1)
                sc.barrier()
            al.reset(ph)

        if "pw" in phases:
            ph = al.scope()
            wo_sb = al.tile([128, NDC, 512])
            bo_sb = al.tile([128, D])
            atb = [al.tile([128, D]) for _ in range(2)]
            afm = al.tile([128, NDC, 128])
            wst = [al.tile([128, 512]) for _ in range(2)]
            bol = sc.dma("sync", "dMISC", bo_sb[:, :], bo_b)
            for fg in range(4):
                wls2 = []
                for dc in range(NDC):
                    wls2.append(
                        sc.dma("sync", "dWO", wo_sb[:, dc, :],
                               wo_r[dc, :, fg * 512:(fg + 1) * 512]))
                wl = wls2[-1]
                wev, wsd, trl = [], [], {}
                trst = {}
                for job in range(NJOB):
                    it = job % 2
                    if job >= 2:
                        sc.wait("sync", "cP", trl[job - 2])
                    a_ld = sc.dma("sync", f"dAT{it}", atb[it][:, :],
                                  at_s[job])
                    sc.wait("pe", f"dAT{it}", a_ld)
                    bank = pbk[2 + job % 2]
                    if job == 0:
                        sc.wait("pe", "dMISC", bol)
                    else:
                        sc.wait("pe", "dWO", wl)
                    if len(wev) >= 2:
                        sc.wait("pe", "cV", wev[-2])
                    for dc in range(NDC):
                        if job == 0:
                            sc.wait("pe", "dWO", wls2[dc])
                        sc.emit("pe", lambda e, d=dc, b=bank, i2=it:
                                e.matmul(b[:, :],
                                         atb[i2][:, d * 128:(d + 1) * 128],
                                         wo_sb[:, d, :],
                                         start=(d == 0), stop=(d == NDC - 1)))
                    p1 = sc.inc("pe", "cP")
                    trl[job] = p1
                    sc.wait("dve", "cP", p1)
                    if len(wsd) >= 2:
                        sc.wait("dve", f"dAF{job % 2}", wsd[-2][1])
                    ob = wst[job % 2]
                    sc.emit("dve", lambda e, b=bank, o=ob, f=fg:
                            e.scalar_tensor_tensor(
                                o[:, :], b[:, :], 1.0,
                                bo_sb[:, f * 512:(f + 1) * 512],
                                op0=ALU.mult, op1=ALU.add))
                    wev.append(sc.inc("dve", "cV"))
                    sc.wait("gp", "cV", wev[-1])
                    wsd.append((job % 2, sc.dma(
                        "gp", f"dAF{job % 2}",
                        af_s[job, :, fg * 512:(fg + 1) * 512], ob[:, :])))
                sc.barrier()
            al.reset(ph)

        if "pf" in phases:
            ph = al.scope()
            xnqh = al.tile([128, NDC, 512], BF16)
            xnql = al.tile([128, NDC, 512], BF16)
            acc = al.tile([128, NDC, 512])
            woff0 = al.scope()
            Wh0 = al.tile([128, 4, NDC, 128], BF16)
            Wl0 = al.tile([128, 4, NDC, 128], BF16)
            woff1 = al.scope()
            Wh1 = al.tile([128, 4, NDC, 128], BF16)
            Wl1 = al.tile([128, 4, NDC, 128], BF16)
            Wh_sb = [Wh0, Wh1]
            Wl_sb = [Wl0, Wl1]
            pr_sb = al.tile_at(woff0, [128, NDC, 512])
            ybuf4 = al.tile_at(woff1, [128, 4, D])
            EXBASE = al.scope()
            hb = [al.tile([128, 512]) for _ in range(2)]
            wbc = al.tile([128, 512])
            wcT8 = al.tile([128, 512])
            lg_fm = al.tile([128, 512])
            lg_tm = al.tile([128, 4, 8])
            mx8 = al.tile([128, 4, 8])
            mr1 = al.tile([128, 4, 8])
            mr2 = al.tile([128, 4, 8])
            row1 = al.tile([128, 4, 8])
            row2 = al.tile([128, 4, 8])
            e1m = al.tile([128, 4, 8])
            e2m = al.tile([128, 4, 8])
            wc = al.tile([128, 4, 8])
            wc2 = al.tile([128, 4, 8])
            w12 = al.tile([128, 4, 4])
            rtmp = al.tile([128, 4])
            wr_sb = al.tile([128, NDC, 8])
            wrh_sb = al.tile([128, NDC, 8], BF16)
            wrl_sb = al.tile([128, NDC, 8], BF16)
            pb_sb = al.tile([128, D])
            nw_sb = al.tile([128, D])
            xob = al.tile([128, D])
            afb = al.tile([128, D])
            ssq = al.tile([128, 4, 4])
            rsm = al.tile([128, 4])
            sdv = al.tile([128, 4])
            rrms = al.tile([128, 4])
            xn32j = al.tile([128, NDC, 128])

            sc.dma("sync", "dMISC", wr_sb[:], wr_r)
            sc.dma("sync", "dMISC", pb_sb[:, :], pb_b)
            misc_l = sc.dma("sync", "dMISC", nw_sb[:, :], nw_b)
            sc.wait("dve", "dMISC", misc_l)
            wrf = wr_sb.rearrange("p a b -> p (a b)")
            wrhf = wrh_sb.rearrange("p a b -> p (a b)")
            wrlf = wrl_sb.rearrange("p a b -> p (a b)")
            sc.emit("dve", lambda e: e.tensor_copy(wrhf, wrf))
            sc.rt("dve")
            sc.emit("dve", lambda e: e.tensor_tensor(wrlf, wrf, wrhf,
                                                     op=ALU.subtract))
            wr_sp = sc.inc("dve", "cV")

            prev_out = []
            for qr in range(4):
                jobs = [qr * 4 + jj for jj in range(4)]
                xl = 0
                for jj, job in enumerate(jobs):
                    for dc in range(NDC):
                        sc.dma("sync", "dXQ",
                               xnqh[:, dc, jj * 128:(jj + 1) * 128],
                               xnh_s[job, dc, :, :])
                        xl = sc.dma("sync", "dXQ",
                                    xnql[:, dc, jj * 128:(jj + 1) * 128],
                                    xnl_s[job, dc, :, :])
                bank = pbk[6]
                bank2 = pbk[7]
                sc.wait("pe", "dXQ", xl)
                if qr == 0:
                    sc.wait("pe", "cV", wr_sp)
                for d3 in range(3 * NDC):
                    dc, p3 = d3 // 3, d3 % 3
                    wr3 = (wrh_sb, wrh_sb, wrl_sb)[p3]
                    xn3 = (xnqh, xnql, xnqh)[p3]
                    sc.emit("pe", lambda e, d=dc, b=bank, w3=wr3, x3=xn3,
                            s0=(d3 == 0), s1=(d3 == 3 * NDC - 1):
                            e.matmul(b[:8, :], w3[:, d, :], x3[:, d, :],
                                     start=s0, stop=s1,
                                     skip_group_check=True))
                p1 = sc.inc("pe", "cP")
                sc.wait("act", "cP", p1)
                sc.emit("act", lambda e, b=bank:
                        e.activation(lg_fm[:8, :], b[:8, :], ACTF.Identity,
                                     bias=c_rb[:8, :]))
                a1 = sc.inc("act", "cA")
                sc.wait("pe", "cA", a1)
                for jj in range(4):
                    sc.emit("pe", lambda e, j2=jj, b=bank2:
                            e.transpose(b[:, j2 * 8:(j2 + 1) * 8],
                                        lg_fm[:8, j2 * 128:(j2 + 1) * 128],
                                        c_ident[:8, :8]))
                p2 = sc.inc("pe", "cP")
                sc.wait("dve", "cP", p2)
                for jj in range(4):
                    sc.emit("dve", lambda e, j2=jj:
                            e.tensor_copy(lg_tm[:, j2, :],
                                          bank2[:, j2 * 8:(j2 + 1) * 8]))
                sc.rt("dve")
                for jj in range(4):
                    sc.emit("dve", lambda e, j2=jj:
                            e.max(out=mx8[:, j2, :], in_=lg_tm[:, j2, :]))
                sc.rt("dve")
                for jj in range(4):
                    sc.emit("dve", lambda e, j2=jj:
                            e.memset(mr1[:, j2, :], NEG_BIG))
                    sc.emit("dve", lambda e, j2=jj:
                            e.memset(mr2[:, j2, :], NEG_BIG))
                sc.rt("dve")
                for jj in range(4):
                    sc.emit("dve", lambda e, j2=jj:
                            e.tensor_copy(mr1[:, j2, 0:1], mx8[:, j2, 0:1]))
                    sc.emit("dve", lambda e, j2=jj:
                            e.tensor_copy(mr2[:, j2, 0:1], mx8[:, j2, 1:2]))
                    sc.emit("dve", lambda e, j2=jj:
                            e.tensor_tensor(w12[:, j2, 0:1], mx8[:, j2, 1:2],
                                            mx8[:, j2, 0:1], op=ALU.subtract))
                v1 = sc.inc("dve", "cV")
                sc.wait("act", "cV", v1)
                for jj in range(4):
                    sc.emit("act", lambda e, j2=jj:
                            e.activation(w12[:, j2, 1:2], w12[:, j2, 0:1],
                                         ACTF.Exp))
                a2 = sc.inc("act", "cA")
                sc.wait("dve", "cA", a2)
                for jj in range(4):
                    sc.emit("dve", lambda e, j2=jj:
                            e.tensor_scalar(rtmp[:, j2:j2 + 1],
                                            w12[:, j2, 1:2], 1.0, None,
                                            op0=ALU.add))
                sc.rt("dve")
                for jj in range(4):
                    sc.emit("dve", lambda e, j2=jj:
                            e.reciprocal(w12[:, j2, 2:3], rtmp[:, j2:j2 + 1]))
                sc.rt("dve")
                for jj in range(4):
                    sc.emit("dve", lambda e, j2=jj:
                            e.tensor_scalar(w12[:, j2, 3:4], w12[:, j2, 2:3],
                                            1.0, -1.0, op0=ALU.subtract,
                                            op1=ALU.mult))
                sc.rt("dve")
                for jj in range(4):
                    sc.emit("dve", lambda e, j2=jj:
                            e.match_replace(out=row1[:, j2, :],
                                            in_to_replace=mr1[:, j2, :],
                                            in_values=lg_tm[:, j2, :],
                                            imm_value=-1.0e38))
                sc.rt("dve")
                for jj in range(4):
                    sc.emit("dve", lambda e, j2=jj:
                            e.tensor_tensor(e1m[:, j2, :], lg_tm[:, j2, :],
                                            row1[:, j2, :], op=ALU.not_equal))
                for jj in range(4):
                    sc.emit("dve", lambda e, j2=jj:
                            e.match_replace(out=row2[:, j2, :],
                                            in_to_replace=mr2[:, j2, :],
                                            in_values=row1[:, j2, :],
                                            imm_value=-1.0e38))
                sc.rt("dve")
                for jj in range(4):
                    sc.emit("dve", lambda e, j2=jj:
                            e.tensor_tensor(e2m[:, j2, :], row1[:, j2, :],
                                            row2[:, j2, :], op=ALU.not_equal))
                sc.rt("dve")
                for jj in range(4):
                    sc.emit("dve", lambda e, j2=jj:
                            e.tensor_tensor(wc[:, j2, :], e1m[:, j2, :],
                                            w12[:, j2, 2:3].to_broadcast(
                                                [128, 8]), op=ALU.mult))
                for jj in range(4):
                    sc.emit("dve", lambda e, j2=jj:
                            e.tensor_tensor(wc2[:, j2, :], e2m[:, j2, :],
                                            w12[:, j2, 3:4].to_broadcast(
                                                [128, 8]), op=ALU.mult))
                sc.rt("dve")
                for jj in range(4):
                    sc.emit("dve", lambda e, j2=jj:
                            e.tensor_tensor(wc[:, j2, :], wc[:, j2, :],
                                            wc2[:, j2, :], op=ALU.add))
                wc_done = sc.inc("dve", "cV")
                sc.wait("pe", "cV", wc_done)
                for jj in range(4):
                    sc.emit("pe", lambda e, j2=jj, b=bank2:
                            e.transpose(b[:8, j2 * 128:(j2 + 1) * 128],
                                        wc[:, j2, :], c_ident[:, :]))
                p3 = sc.inc("pe", "cP")
                sc.wait("dve", "cP", p3)
                sc.emit("dve", lambda e, b=bank2:
                        e.tensor_copy(wcT8[:8, :], b[:8, :]))
                wct_done = sc.inc("dve", "cV")

                he_p1, he_act, cb_ev, cb_l = [], [], [], []
                acc_cb = {}
                wls = {}

                def pf_wload(e_, fcg_):
                    bi = (e_ * 4 + fcg_) % 2
                    if he_p1:
                        sc.wait("sync", "cP", he_p1[-1])
                    hflat = Wh_sb[bi].rearrange("p a b c -> p (a b c)")
                    lflat = Wl_sb[bi].rearrange("p a b c -> p (a b c)")
                    sc.dma("sync", f"dWE{bi}", hflat, ewh_s[e_, fcg_])
                    wls[(e_, fcg_)] = sc.dma("sync", f"dWE{bi}", lflat,
                                             ewl_s[e_, fcg_])

                pf_wload(0, 0)
                cb_last = 0
                for e_ in range(8):
                    bbank = pbk[6]
                    sc.wait("pe", "cV", wct_done)
                    if cb_ev:
                        sc.wait("pe", "cV", cb_ev[-1])
                    sc.emit("pe", lambda e, x=e_, b=bbank:
                            e.matmul(b[:, :], c_esel[0:8, x, :], wcT8[:8, :],
                                     start=True, stop=True,
                                     skip_group_check=True))
                    pb1 = sc.inc("pe", "cP")
                    sc.wait("dve", "cP", pb1)
                    if cb_l:
                        sc.wait("dve", "cV", cb_l[-1])
                    sc.emit("dve", lambda e, b=bbank:
                            e.tensor_copy(wbc[:, :], b[:, :]))
                    cb_ev.append(sc.inc("dve", "cV"))
                    for fcg in range(4):
                        bi = (e_ * 4 + fcg) % 2
                        nxt = e_ * 4 + fcg + 1
                        if nxt < 32:
                            pf_wload(nxt // 4, nxt % 4)
                        sc.wait("pe", f"dWE{(e_ * 4 + fcg) % 2}",
                                wls[(e_, fcg)])
                        for fi in range(4):
                            fc = fcg * 4 + fi
                            bank = pbk[fc % 2]
                            if len(he_act) >= 2:
                                sc.wait("pe", "cA", he_act[-2])
                            for d3 in range(3 * NDC):
                                dc, p3 = d3 // 3, d3 % 3
                                xn3 = (xnqh, xnql, xnqh)[p3]
                                w3 = (Wh_sb[bi], Wh_sb[bi], Wl_sb[bi])[p3]
                                sc.emit("pe", lambda e, d=dc, b=bank,
                                        w=w3, x3=xn3, f2=fi,
                                        s0=(d3 == 0),
                                        s1=(d3 == 3 * NDC - 1):
                                        e.matmul(b[:, :], w[:, f2, d, :],
                                                 x3[:, d, :],
                                                 start=s0, stop=s1,
                                                 skip_group_check=True))
                            he_p1.append(sc.inc("pe", "cP"))
                            sc.wait("act", "cP", he_p1[-1])
                            if len(cb_l) >= 2:
                                sc.wait("act", "cV", cb_l[-2])
                            hbb = hb[fc % 2]
                            sc.emit("act", lambda e, b=bank, h=hbb, x=e_, f=fc:
                                    e.activation(h[:, :], b[:, :], ACTF.Silu,
                                                 bias=c_ebpp[:, x, f:f + 1]))
                            he_act.append(sc.inc("act", "cA"))
                            sc.wait("dve", "cA", he_act[-1])
                            sc.wait("dve", "cV", cb_ev[-1])
                            if fc in acc_cb:
                                sc.wait("dve", "cV", acc_cb[fc])
                            if e_ == 0:
                                sc.emit("dve", lambda e, h=hbb, f=fc:
                                        e.tensor_tensor(acc[:, f, :], h[:, :],
                                                        wbc[:, :],
                                                        op=ALU.mult))
                            else:
                                sc.emit("dve", lambda e, h=hbb:
                                        e.tensor_tensor(h[:, :], h[:, :],
                                                        wbc[:, :],
                                                        op=ALU.mult))
                                sc.rt("dve")
                                sc.emit("dve", lambda e, h=hbb, f=fc:
                                        e.tensor_tensor(acc[:, f, :],
                                                        acc[:, f, :],
                                                        h[:, :], op=ALU.add))
                            cb_last = sc.inc("dve", "cV")
                            cb_l.append(cb_last)
                            acc_cb[fc] = cb_last

                last_he = he_p1[-1]
                sc.wait("pe", "cV", cb_last)
                sc.wait("pe", "dMISC", misc_l)
                ytr_hi = []
                xr_done = []
                sc.wait("dve", "dXQ", xl)
                for jj in range(4):
                    if xr_done:
                        sc.wait("dve", "cP", ytr_p1[-1])
                    sc.emit("dve", lambda e, j2=jj:
                            e.tensor_tensor(
                                xn32j[:, :, :],
                                xnqh[:, :, j2 * 128:(j2 + 1) * 128],
                                xnql[:, :, j2 * 128:(j2 + 1) * 128],
                                op=ALU.add))
                    xr = sc.inc("dve", "cV")
                    xr_done.append(xr)
                    sc.wait("pe", "cV", xr)
                    for half in range(4):
                        bank = pbk[2 + half % 2]
                        if len(ytr_hi) >= 2:
                            sc.wait("pe", "cA", ytr_hi[-2])
                        for q in range(4):
                            dc = half * 4 + q
                            sc.emit("pe", lambda e, d=dc, q2=q, b=bank:
                                    e.transpose(
                                        b[:, q2 * 128:(q2 + 1) * 128],
                                        xn32j[:, d, :],
                                        c_ident[:, :]))
                        ph1 = sc.inc("pe", "cP")
                        ytr_p1 = [ph1]
                        sc.wait("act", "cP", ph1)
                        for q in range(4):
                            dc = half * 4 + q
                            sc.emit("act", lambda e, d=dc, q2=q, b=bank,
                                    j2=jj:
                                    _copy(e,
                                          ybuf4[:, j2, d * 128:(d + 1) * 128],
                                          b[:, q2 * 128:(q2 + 1) * 128]))
                        ytr_hi.append(sc.inc("act", "cA"))
                ytr_last = ytr_hi[-1]
                sc.wait("dve", "cA", ytr_last)
                sc.wait("dve", "dMISC", misc_l)
                for jj in range(4):
                    sc.emit("dve", lambda e, j2=jj:
                            e.tensor_tensor(ybuf4[:, j2, :], ybuf4[:, j2, :],
                                            pb_sb[:, :], op=ALU.add))
                ypb_done = sc.inc("dve", "cV")
                prj_ev, prj_p1 = [], []
                for fg in range(4):
                    sc.wait("sync", "cP",
                            last_he if fg == 0 else prj_p1[-1])
                    prls = []
                    for fc in range(NDC):
                        prls.append(
                            sc.dma("sync", "dPR", pr_sb[:, fc, :],
                                   pr_r[fc, :, fg * 512:(fg + 1) * 512]))
                    prl = prls[-1]
                    for jj in range(4):
                        bank = pbk[4 + jj % 2]
                        if jj > 0:
                            sc.wait("pe", "dPR", prl)
                        if len(prj_ev) >= 2:
                            sc.wait("pe", "cV", prj_ev[-2])
                        for fc in range(NDC):
                            if jj == 0:
                                sc.wait("pe", "dPR", prls[fc])
                            sc.emit("pe", lambda e, f=fc, j2=jj, b=bank:
                                    e.matmul(b[:, :],
                                             acc[:, f, j2 * 128:(j2 + 1) * 128],
                                             pr_sb[:, f, :],
                                             start=(f == 0),
                                             stop=(f == NDC - 1)))
                        prj_p1.append(sc.inc("pe", "cP"))
                        sc.wait("dve", "cP", prj_p1[-1])
                        if fg == 0 and jj == 0:
                            sc.wait("dve", "cV", ypb_done)
                        sc.emit("dve", lambda e, j2=jj, b=bank, f=fg:
                                e.scalar_tensor_tensor(
                                    ybuf4[:, j2, f * 512:(f + 1) * 512],
                                    b[:, :], 1.0,
                                    ybuf4[:, j2, f * 512:(f + 1) * 512],
                                    op0=ALU.mult, op1=ALU.add))
                        prj_ev.append(sc.inc("dve", "cV"))
                sc.wait("act", "cV", prj_ev[-1])
                for jj in range(4):
                    for c4 in range(4):
                        sc.emit("act", lambda e, j2=jj, c2=c4:
                                e.activation(pbk[7][:, :],
                                             ybuf4[:, j2,
                                                   c2 * 512:(c2 + 1) * 512],
                                             ACTF.Square,
                                             accum_out=ssq[:, j2, c2:c2 + 1]))
                        sc.rt("act")
                aq = sc.inc("act", "cA")
                sc.wait("dve", "cA", aq)
                for jj in range(4):
                    sc.emit("dve", lambda e, j2=jj:
                            e.reduce_sum(rsm[:, j2:j2 + 1], ssq[:, j2, :],
                                         axis=AX))
                sc.rt("dve")
                for jj in range(4):
                    sc.emit("dve", lambda e, j2=jj:
                            e.tensor_scalar(rsm[:, j2:j2 + 1],
                                            rsm[:, j2:j2 + 1], 1.0 / D, None,
                                            op0=ALU.mult))
                v2 = sc.inc("dve", "cV")
                sc.wait("act", "cV", v2)
                for jj in range(4):
                    sc.emit("act", lambda e, j2=jj:
                            e.activation(sdv[:, j2:j2 + 1], rsm[:, j2:j2 + 1],
                                         ACTF.Sqrt, bias=eps_rms))
                a3 = sc.inc("act", "cA")
                sc.wait("dve", "cA", a3)
                for jj in range(4):
                    sc.emit("dve", lambda e, j2=jj:
                            e.reciprocal(rrms[:, j2:j2 + 1], sdv[:, j2:j2 + 1]))
                rr_done = sc.inc("dve", "cV")
                sc.wait("dve", "dMISC", misc_l)
                sc.wait("dve", "cV", rr_done)
                for jj in range(4):
                    job = qr * 4 + jj
                    if prev_out:
                        sc.wait("sync", "dO", prev_out[-1])
                    xl2 = sc.dma("sync", "dTL", xob[:, :], x_own[job])
                    al2 = sc.dma("sync", "dTL", afb[:, :], af_s[job])
                    sc.emit("dve", lambda e, j2=jj:
                            e.tensor_tensor(ybuf4[:, j2, :], ybuf4[:, j2, :],
                                            rrms[:, j2:j2 + 1].to_broadcast(
                                                [128, D]), op=ALU.mult))
                    sc.rt("dve")
                    sc.emit("dve", lambda e, j2=jj:
                            e.tensor_tensor(ybuf4[:, j2, :], ybuf4[:, j2, :],
                                            nw_sb[:, :], op=ALU.mult))
                    sc.wait("dve", "dTL", al2)
                    sc.emit("dve", lambda e:
                            e.tensor_tensor(xob[:, :], xob[:, :], afb[:, :],
                                            op=ALU.add))
                    sc.rt("dve")
                    sc.emit("dve", lambda e, j2=jj:
                            e.tensor_tensor(xob[:, :], xob[:, :],
                                            ybuf4[:, j2, :], op=ALU.add))
                    fo = sc.inc("dve", "cV")
                    sc.wait("gp", "cV", fo)
                    prev_out.append(
                        sc.dma("gp", "dO", out_own[job], xob[:, :]))
                sc.barrier()
            al.reset(ph)

        sc.wait("gp", "dO", sc.cnt["dO"])
        sc.barrier()

        with nc.Block() as block:
            @block.tensor
            def _(eng):
                for f in sc.ops["pe"]:
                    f(eng)

            @block.vector
            def _(eng):
                for f in sc.ops["dve"]:
                    f(eng)

            @block.scalar
            def _(eng):
                for f in sc.ops["act"]:
                    f(eng)

            @block.sync
            def _(eng):
                for f in sc.ops["sync"]:
                    f(eng)

            @block.gpsimd
            def _(eng):
                for f in sc.ops["gp"]:
                    f(eng)

    return nc


def make_inputs(x, wq, bq, wk, bk, wv, bv, wo, bo, router_w, router_b,
                expert_w, expert_b, proj_w, proj_b, norm_w):
    f = np.float32
    common = {}
    common["wq_r"] = np.ascontiguousarray(
        wq.reshape(D, NH, HD).transpose(1, 0, 2), f)
    common["wk_r"] = np.ascontiguousarray(
        wk.reshape(D, NKV, HD).transpose(1, 0, 2), f)
    common["wv_r"] = np.ascontiguousarray(
        wv.reshape(NDC, 128, NKV * HD).transpose(1, 0, 2), f)
    common["wo_r"] = np.ascontiguousarray(wo.reshape(NDC, 128, D), f)
    common["wr_r"] = np.ascontiguousarray(
        router_w.reshape(NDC, 128, 8).transpose(1, 0, 2), f)
    common["ew_r"] = np.ascontiguousarray(
        expert_w.reshape(8, NDC, 128, 4, 4, 128).transpose(0, 3, 2, 4, 1, 5), f)
    common["pr_r"] = np.ascontiguousarray(proj_w.reshape(NDC, 128, D), f)
    common["eb_pp"] = np.ascontiguousarray(
        expert_b.reshape(8, NDC, 128).transpose(2, 0, 1), f)
    es = np.zeros((128, 8, 128), f)
    for e in range(8):
        es[e, e, :] = 1.0
    common["esel"] = es
    common["bq_p"] = np.ascontiguousarray(bq.reshape(NH, 128).T, f)
    common["bk_p"] = np.ascontiguousarray(bk.reshape(NKV, 128).T, f)
    common["bv_b"] = np.ascontiguousarray(np.tile(bv[None, :], (128, 1)), f)
    common["bo_b"] = np.ascontiguousarray(np.tile(bo[None, :], (128, 1)), f)
    common["pb_b"] = np.ascontiguousarray(np.tile(proj_b[None, :], (128, 1)), f)
    common["nw_b"] = np.ascontiguousarray(np.tile(norm_w[None, :], (128, 1)), f)
    rb = np.zeros((128, 1), f)
    rb[:8, 0] = router_b
    common["rb_p"] = rb
    common["ident"] = np.eye(128, dtype=f)
    import ml_dtypes
    common["identb"] = np.eye(128).astype(ml_dtypes.bfloat16)
    cp = np.zeros((128, 8), f)
    cp[:, 0] = LN_EPS
    cp[:, 1] = RMS_EPS
    cp[:, 2] = NEG_BIG
    common["cpool"] = cp

    in_maps = []
    for core in range(8):
        b, r = core // 2, core % 2
        m = dict(common)
        m["x_full"] = np.ascontiguousarray(x[b], f)
        xo = np.empty((NJOB, 128, D), f)
        mk = np.zeros((NJOB, 128, 512), f)
        for job in range(NJOB):
            cls = job_class(r, job)
            xo[job] = x[b, cls * 128:(cls + 1) * 128, :]
            j = job // 2
            key = 512 * j + np.arange(512)[None, :]
            row = cls * 128 + np.arange(128)[:, None]
            mk[job] = np.where(key > row, MASK_ADD, 0.0).astype(f)
        m["x_own"] = xo
        m["masks"] = mk
        in_maps.append(m)
    return in_maps


_NC_CACHE = {}


def kernel(L0_attn_x, L0_attn_wq, L0_attn_bq, L0_attn_wk, L0_attn_bk,
           L0_attn_wv, L0_attn_bv, L0_attn_wo, L0_attn_bo,
           L0_ffn_router_w, L0_ffn_router_b, L0_ffn_expert_w,
           L0_ffn_expert_b, L0_ffn_proj_w, L0_ffn_proj_b, L0_ffn_norm_w):
    x = np.asarray(L0_attn_x, np.float32)
    in_maps = make_inputs(
        x, np.asarray(L0_attn_wq, np.float32), np.asarray(L0_attn_bq, np.float32),
        np.asarray(L0_attn_wk, np.float32), np.asarray(L0_attn_bk, np.float32),
        np.asarray(L0_attn_wv, np.float32), np.asarray(L0_attn_bv, np.float32),
        np.asarray(L0_attn_wo, np.float32), np.asarray(L0_attn_bo, np.float32),
        np.asarray(L0_ffn_router_w, np.float32),
        np.asarray(L0_ffn_router_b, np.float32),
        np.asarray(L0_ffn_expert_w, np.float32),
        np.asarray(L0_ffn_expert_b, np.float32),
        np.asarray(L0_ffn_proj_w, np.float32),
        np.asarray(L0_ffn_proj_b, np.float32),
        np.asarray(L0_ffn_norm_w, np.float32))
    if "nc" not in _NC_CACHE:
        _NC_CACHE["nc"] = build_program()
    nc = _NC_CACHE["nc"]
    res = run_bass_kernel_spmd(nc, in_maps, core_ids=list(range(8)))
    out = np.empty((B, S, D), np.float32)
    for core in range(8):
        b, r = core // 2, core % 2
        oo = np.asarray(res.results[core]["out_own"])
        for job in range(NJOB):
            cls = job_class(r, job)
            out[b, cls * 128:(cls + 1) * 128, :] = oo[job]
    return out
